# version 10
# baseline (speedup 1.0000x reference)
import numpy as np
import ml_dtypes
from contextlib import ExitStack

import concourse.bass as bass
import concourse.mybir as mybir
from concourse.bass import AP
from concourse.bass_utils import run_bass_kernel_spmd

F32 = mybir.dt.float32
BF16 = mybir.dt.bfloat16
AF = mybir.ActivationFunctionType
ALU = mybir.AluOpType
AX = mybir.AxisListType

MAGIC = float(np.float32(1.5 * 2.0 ** 23))
NEG = -1.0e30
EPS = 1e-5
R127 = float(np.float32(1.0 / 127.0))

DM, DI, NH, HD, DS = 1024, 2048, 32, 64, 128
CONVD, DIP, FFN = 2304, 4480, 4096
T, NBLK, BLK, NCHK, CHK = 2048, 4, 512, 16, 128
KD = DM // 128
ENG = ("t", "v", "s", "g", "q")

TAPS = []


class Sem:
    __slots__ = ("h", "n", "name")

    def __init__(self, h, name):
        self.h, self.n, self.name = h, 0, name


class KB:
    def __init__(self, nc, stack):
        self.nc, self.stack = nc, stack
        self.streams = {e: [] for e in ENG}
        self.nins = 0

    def sem(self, name):
        return Sem(self.stack.enter_context(self.nc.semaphore(name)), name)

    def emit(self, eng, fn):
        self.streams[eng].append(fn)
        self.nins += 1

    def op(self, eng, fn, inc=None, n=1):
        if inc is not None:
            inc.n += n
            self.emit(eng, lambda e, fn=fn, h=inc.h, n=n: fn(e).then_inc(h, n))
        else:
            self.emit(eng, fn)

    def dma(self, eng, fn, inc):
        self.op(eng, fn, inc=inc, n=16)

    def wait(self, eng, sem, val=None):
        v = sem.n if val is None else val
        if v > 0:
            self.emit(eng, lambda e, h=sem.h, v=v: e.wait_ge(h, v))

    def vsync(self, sv, ss):
        self.wait("s", sv)
        self.op("s", lambda s: s.nop(), inc=ss)
        self.wait("v", ss)

    def barrier(self, sem, engines=ENG, dma_sems=()):
        base = sem.n
        for e in engines:
            self.op(e, lambda en: en.nop(), inc=sem)
        for e in engines:
            self.wait(e, sem, base + len(engines))
            for ds in dma_sems:
                self.wait(e, ds)


def bap(t, dims, offset=0):
    return AP(t, offset, [list(p) for p in dims])


def build_nc():
    nc = bass.Bass("TRN2", target_bir_lowering=False)
    dt = {}

    def din(name, shape, dtype=F32):
        dt[name] = nc.dram_tensor(name, shape, dtype, kind="ExternalInput")
        return dt[name]

    def dscr(name, shape, dtype):
        dt[name] = nc.dram_tensor(name, shape, dtype)
        return dt[name]

    x_in = din("x", [T, DM])
    din("xh", [3, DM])
    din("hm", [1, 1])
    din("mmat", [8, 8])
    din("mmask", [8, 1])
    w_in = din("w_in", [DM, DIP])
    wo_in = din("wo", [DI, DM])
    w1_in = din("w1", [DM, FFN])
    w2_in = din("w2", [FFN, DM])
    din("convw", [128, 72])
    din("convb", [128, 18])
    din("lnw", [128, KD])
    din("lnb", [128, KD])
    din("dtb", [32, 1])
    din("alog", [32, 1])
    din("dvec", [1, 32])
    din("nw", [128, 16])
    din("b1c", [128, 32])
    din("b2c", [128, 8])
    out_t = nc.dram_tensor("out", [T, DM], F32, kind="ExternalOutput")

    d_xn = dscr("d_xn", [T, DM], BF16)
    d_x2 = dscr("d_x2", [T, DM], BF16)
    d_z = dscr("d_z", [DI, T], BF16)
    d_xc = dscr("d_xc", [DI, T], BF16)
    d_sr = dscr("d_sr", [32, T], F32)
    d_srf = dscr("d_srf", [1, NCHK * 4096], F32)
    d_h = dscr("d_h", [FFN, T], F32)
    d_xnf = dscr("d_xnf", [T, DM], F32)
    d_xq = dscr("d_xq", [T, DM], BF16)
    d_yr = dscr("d_yr", [T, DI], BF16)
    d_F = dscr("d_F", [DM, T], BF16)
    d_sdbg = dscr("d_sdbg", [DM, T], BF16)
    d_val = dscr("d_val", [1, 16], F32)
    d_coef = dscr("d_coef", [8, 32], F32)
    cA_in = dscr("cA_in", [16, 128], F32)
    cA_out = dscr("cA_out", [128, 128], F32)
    cB_in = dscr("cB_in", [16, 128], F32)
    cB_out = dscr("cB_out", [128, 128], F32)
    cC_in = dscr("cC_in", [16, 128], F32)
    cC_out = dscr("cC_out", [128, 128], F32)
    cD_in = dscr("cD_in", [128, DI], BF16)
    cD_out = dscr("cD_out", [512, DI], BF16)

    tap_windows = {"xnt0": (d_xn, 0), "xnt1": (d_xn, 128), "xnt15": (d_xn, 1920)}
    tap_shapes = {"xn": ([T, DM], d_xn, BF16), "x2": ([T, DM], d_x2, BF16),
                  "z": ([DI, T], d_z, BF16), "xc": ([DI, T], d_xc, BF16),
                  "srel": ([32, T], d_sr, F32), "h": ([FFN, T], d_h, F32),
                  "yr": ([T, DI], d_yr, BF16), "F": ([DM, T], d_F, BF16),
                  "sres": ([DM, T], d_sdbg, BF16),
                  "hseg": ([128, DI], cD_in, BF16), "val": ([1, 16], d_val, F32)}
    taps = {}
    for name in TAPS:
        if name in tap_windows:
            tsrc, row = tap_windows[name]
            taps[name] = (nc.dram_tensor("tap_" + name, [128, DM], BF16,
                                         kind="ExternalOutput"),
                          tsrc, row)
            continue
        shp, src, tdt = tap_shapes[name]
        taps[name] = (nc.dram_tensor("tap_" + name, shp, tdt, kind="ExternalOutput"), src)

    stack = ExitStack()
    with stack:
        kb = KB(nc, stack)
        sb = lambda *a, **k: stack.enter_context(nc.sbuf_tensor(*a, **k))
        psu = lambda *a, **k: stack.enter_context(nc.psum_tensor(*a, **k))

        pA = psu("pA", [128, 512], F32)
        pB = psu("pB", [128, 512], F32)
        pW = psu("pW", [128, 2048], F32)
        pH = psu("pH", [128, 384], F32)
        pS = psu("pS", [128, 512], F32)

        S = lambda n: kb.sem(n)
        s_t, s_v, s_s, s_g, s_bar = S("s_t"), S("s_v"), S("s_s"), S("s_g"), S("s_bar")
        d_small, d_win, d_x, d_tr, d_w = S("d_small"), S("d_win"), S("d_x"), S("d_tr"), S("d_w")
        d_sp0, d_sp1 = S("d_sp0"), S("d_sp1")
        d_ld0, d_ld1 = S("d_ld0"), S("d_ld1")
        d_cc = S("d_cc")
        d_x2t = [S("d_x2t0"), S("d_x2t1")]
        d_xtk = [S("d_xtk0"), S("d_xtk1")]
        c_A, c_B, c_C, c_D = S("c_A"), S("c_B"), S("c_C"), S("c_D")
        s_out, s_tap = S("s_out"), S("s_tap")
        evA, evB, evW, evS = S("evA"), S("evB"), S("evW"), S("evS")

        idn_b = sb("idn_b", [128, 128], BF16)
        idn_f = sb("idn_f", [128, 128], F32)
        onesb = sb("onesb", [1, 128], BF16)
        BT = sb("BT", [DS, T], BF16)
        CTt = sb("CTt", [DS, T], BF16)
        srel = sb("srel", [32, T], F32)
        ldms = sb("ldms", [32, T], F32)
        elog = sb("elog", [32, 1], F32)
        dtsc = sb("dtsc", [32, 4], F32)
        hst = sb("hst", [DS, DI], BF16)
        DT32 = sb("DT32", [128, 32], BF16)
        cwgt = sb("cwgt", [128, 72], F32)
        cbia = sb("cbia", [128, 18], F32)
        lnwc = sb("lnwc", [128, KD], F32)
        lnbc = sb("lnbc", [128, KD], F32)
        lnbb = sb("lnbb", [128, KD], BF16)
        dtbc = sb("dtbc", [32, 1], F32)
        alogc = sb("alogc", [32, 1], F32)
        nac = sb("nac", [32, 1], F32)
        nwc = sb("nwc", [128, 16], F32)
        b1sb = sb("b1sb", [128, 32], F32)
        b2sb = sb("b2sb", [128, 8], F32)
        biasj = sb("biasj", [128, 35], F32)
        b34 = sb("b34", [32, 1], F32)
        mmatb = sb("mmatb", [8, 8], BF16)
        mmaskc = sb("mmaskc", [8, 1], F32)
        mres = sb("mres", [32, BLK], F32)
        vals = sb("vals", [1, 16], F32)
        cols = sb("cols", [128, 8], F32)
        maxxn = sb("maxxn", [128, 1], F32)
        mx16 = sb("mx16", [128, 16], F32)
        wm32 = sb("wm32", [128, 32], F32)

        maxh = sb("maxh", [128, 1], F32)
        b1s = sb("b1s", [128, 32], F32)
        b2s = sb("b2s", [128, 8], F32)
        stats = sb("stats", [128, 12], F32)
        kconst = sb("kconst", [128, 3], F32)
        nc.const_aps.register(F32, EPS, kconst[:, 0:1].opt())
        nc.const_aps.register(F32, 1.0, kconst[:, 1:2].opt())
        nc.const_aps.register(F32, MAGIC, kconst[:, 2:3].opt())
        hstat = sb("hstat", [3, 12], F32)
        payl = sb("payl", [16, 128], F32)

        def ln_tile(src_ap_fn, p, fd, out_fn, out2_fn=None, absmax_col=None,
                    out32_fn=None):
            st = stats if p == 128 else hstat
            kb.op("v", lambda v: v.tensor_reduce(st[0:p, 0:1], src_ap_fn(), AX.X,
                                                 ALU.add), inc=s_v)
            kb.wait("s", s_v)
            kb.op("s", lambda s: s.activation(sq_scr[0:p, 0:fd], src_ap_fn(), AF.Square,
                                              accum_out=st[0:p, 1:2]))
            kb.op("s", lambda s: s.activation(st[0:p, 3:4], st[0:p, 0:1], AF.Square,
                                              scale=1.0 / fd), inc=s_s)
            kb.wait("v", s_s)
            kb.op("v", lambda v: v.scalar_tensor_tensor(
                st[0:p, 4:5], st[0:p, 1:2], 1.0 / fd, st[0:p, 3:4],
                ALU.mult, ALU.subtract), inc=s_v)
            kb.wait("s", s_v)
            kb.op("s", lambda s: s.activation(st[0:p, 5:6], st[0:p, 4:5], AF.Ln, bias=EPS))
            kb.op("s", lambda s: s.activation(st[0:p, 2:3], st[0:p, 0:1], AF.Copy,
                                              scale=1.0 / fd))
            kb.op("s", lambda s: s.activation(st[0:p, 6:7], st[0:p, 5:6], AF.Exp,
                                              scale=-0.5), inc=s_s)
            kb.wait("v", s_s)
            if out32_fn is None:
                kb.op("v", lambda v: v.tensor_scalar(out_fn(), src_ap_fn(), st[0:p, 2:3],
                                                     st[0:p, 6:7], ALU.subtract, ALU.mult),
                      inc=s_v)
            else:
                kb.op("v", lambda v: v.tensor_scalar(out32_fn(), src_ap_fn(), st[0:p, 2:3],
                                                     st[0:p, 6:7], ALU.subtract, ALU.mult))
                kb.op("v", lambda v: v.tensor_copy(out_fn(), out32_fn()), inc=s_v)
                if absmax_col is not None:
                    kb.op("v", lambda v: v.reduce_max(absmax_col, out32_fn(), AX.X,
                                                      apply_absolute_value=True), inc=s_v)
            if out2_fn is not None:
                ln_tile(out32_fn if out32_fn is not None else out_fn, p, fd, out2_fn)

        def part_max_to_val(col, slot):
            kb.wait("t", s_v)
            kb.op("t", lambda t: t.transpose(pH[0:1, 0:128], col[:, :], idn_f[:, :]), inc=s_t)
            kb.wait("v", s_t)
            kb.op("v", lambda v: v.tensor_reduce(vals[0:1, slot:slot + 1], pH[0:1, 0:128],
                                                 AX.X, ALU.max), inc=s_v)

        sq_scr = sb("sq_scr", [128, 2048], F32)
        phT = ExitStack()
        tcolA = phT.enter_context(nc.sbuf_tensor("tcolA", [128, NCHK * 32], F32))
        estA = phT.enter_context(nc.sbuf_tensor("estA", [128, NCHK * 32], F32))
        ebcA = phT.enter_context(nc.sbuf_tensor("ebcA", [128, NCHK * 32], BF16))
        e2tA = phT.enter_context(nc.sbuf_tensor("e2tA", [128, NCHK * 32], BF16))
        btokA = phT.enter_context(nc.sbuf_tensor("btokA", [128, NCHK * 128], BF16))
        phW = ExitStack()
        W = phW.enter_context(nc.sbuf_tensor("W", [128, KD * DIP], BF16))
        hal = phW.enter_context(nc.sbuf_tensor("hal", [128, 18 * 3], BF16))
        xhal = phW.enter_context(nc.sbuf_tensor("xhal", [3, DM], F32))
        xh1 = phW.enter_context(nc.sbuf_tensor("xh1", [3, DM], F32))
        xh2 = phW.enter_context(nc.sbuf_tensor("xh2", [3, DM], F32))
        uhT = phW.enter_context(nc.sbuf_tensor("uhT", [128, KD * 3], BF16))

        for nm, tgt in [("convw", cwgt), ("convb", cbia), ("lnw", lnwc),
                        ("lnb", lnbc), ("dtb", dtbc), ("alog", alogc), ("nw", nwc),
                        ("b1c", b1sb), ("b2c", b2sb), ("mmask", mmaskc)]:
            kb.dma("g", lambda g, nm=nm, tgt=tgt: g.dma_start(out=tgt[:, :], in_=dt[nm].ap()),
                   inc=d_small)
        kb.dma("g", lambda g: g.dma_start(out=mmatb[:, :], in_=dt["mmat"].ap()), inc=d_small)
        kb.dma("g", lambda g: g.dma_start(
            out=DT32[:, :], in_=bap(dt["dvec"].ap().tensor, [[0, 128], [1, 32]])), inc=d_small)
        kb.dma("g", lambda g: g.dma_start(
            out=cols[:, 6:7], in_=bap(dt["hm"].ap().tensor, [[0, 128], [1, 1]])), inc=d_small)
        kb.dma("g", lambda g: g.dma_start(out=xhal[:, :], in_=dt["xh"].ap()), inc=d_small)
        for k in range(KD):
            kb.dma("g", lambda g, k=k: g.dma_start(
                out=W[:, k * DIP:(k + 1) * DIP], in_=w_in[k * 128:(k + 1) * 128, :]), inc=d_win)

        kb.wait("g", d_small)
        kb.op("g", lambda g: g.memset(kconst[:, 0:1], EPS))
        kb.op("g", lambda g: g.memset(kconst[:, 1:2], 1.0))
        kb.op("g", lambda g: g.memset(kconst[:, 2:3], MAGIC))
        kb.op("g", lambda g: g.memset(onesb[:, :], 1.0))
        kb.op("g", lambda g: g.memset(idn_f[:, :], 1.0))
        kb.op("g", lambda g: g.affine_select(idn_f[:, :], idn_f[:, :], [[-1, 128]],
                                             ALU.is_equal, 0.0, base=0, channel_multiplier=1))
        kb.op("g", lambda g: g.memset(mres[:, :], 1.0))
        kb.op("g", lambda g: g.memset(
            bap(mres.ap().tensor, [[mres.ap().ap[0][0], 32], [CHK, NBLK], [1, 1]],
                offset=mres.ap().offset), 0.0))
        kb.op("g", lambda g: g.memset(elog[:, :], 0.0))
        kb.op("g", lambda g: g.memset(maxxn[:, :], 0.0))
        kb.op("g", lambda g: g.memset(maxh[:, :], 0.0))
        kb.op("g", lambda g: g.memset(hst[:, :], 0.0))
        kb.op("g", lambda g: g.memset(payl[:, :], 0.0))
        kb.op("g", lambda g: g.memset(vals[:, :], 0.0), inc=s_g)
        g_const = s_g.n
        kb.wait("v", s_g, g_const)
        kb.op("v", lambda v: v.tensor_copy(idn_b[:, :], idn_f[:, :]))
        kb.op("v", lambda v: v.tensor_copy(lnbb[:, :], lnbc[:, :]), inc=s_v)
        v_prep = s_v.n
        kb.wait("s", s_g, g_const)
        kb.op("s", lambda s: s.activation(nac[:, :], alogc[:, :], AF.Exp), inc=s_s)
        kb.wait("v", s_s)
        kb.op("v", lambda v: v.tensor_scalar_mul(nac[:, :], nac[:, :], -1.0), inc=s_v)

        kb.wait("t", d_win)
        kb.wait("t", s_v, v_prep)
        for k in range(KD):
            for j in range(35):
                first = (k == 0 and j == 0)
                last = (k == KD - 1 and j == 34)
                kb.op("t", lambda t, k=k, j=j, f=first, l=last: t.matmul(
                    pA[:, j:j + 1], W[:, k * DIP + j * 128:k * DIP + (j + 1) * 128],
                    lnbb[:, k:k + 1], start=f, stop=l), inc=(s_t if last else None))
        kb.wait("v", s_t)
        kb.op("v", lambda v: v.tensor_add(b34[:, :], pA[0:32, 34:35], dtbc[:, :]))
        kb.op("v", lambda v: v.tensor_copy(biasj[:, :], pA[:, 0:35]), inc=evA)
        kb.op("v", lambda v: v.nop(), inc=s_v)
        for k in range(KD):
            kb.op("v", lambda v, k=k: v.tensor_scalar_mul(
                W[:, k * DIP:(k + 1) * DIP], W[:, k * DIP:(k + 1) * DIP], lnwc[:, k:k + 1]),
                inc=s_v)
        v_wsc = s_v.n

        kb.wait("v", d_small)
        ln_tile(lambda: xhal[:, :], 3, DM, lambda: xh1[:, :],
                out2_fn=lambda: xh2[:, :])
        v_halo_ln = s_v.n
        kb.wait("t", s_v, v_halo_ln)
        kb.wait("t", evA)
        for k in range(KD):
            kb.op("t", lambda t, k=k: t.transpose(
                pH[:, 352:355],
                xh2[0:3, k * 128:(k + 1) * 128], idn_f[0:3, 0:3]), inc=s_t)
            kb.wait("v", s_t)
            kb.op("v", lambda v, k=k: v.tensor_copy(uhT[:, k * 3:(k + 1) * 3],
                                                    pH[:, 352:355]), inc=s_v)
            kb.wait("t", s_v)
        kb.wait("t", s_v, v_wsc)
        for ji, j in enumerate(range(16, 34)):
            for k in range(KD):
                kb.op("t", lambda t, j=j, k=k, f=(k == 0), l=(k == KD - 1): t.matmul(
                    pA[:, 40:43],
                    W[:, k * DIP + j * 128:k * DIP + (j + 1) * 128],
                    uhT[:, k * 3:(k + 1) * 3], start=f, stop=l),
                    inc=(s_t if k == KD - 1 else None))
            kb.wait("v", s_t)
            kb.op("v", lambda v, ji=ji, j=j: v.tensor_scalar(
                hal[:, ji * 3:(ji + 1) * 3], pA[:, 40:43], biasj[:, j:j + 1],
                cols[:, 6:7], ALU.add, ALU.mult), inc=s_v)
            kb.wait("t", s_v)
        v_halo = s_v.n

        ph1 = ExitStack()
        xt = [ph1.enter_context(nc.sbuf_tensor(f"xt{i}", [128, DM], F32)) for i in range(2)]
        xnb = [ph1.enter_context(nc.sbuf_tensor(f"xnb{i}", [128, DM], BF16)) for i in range(2)]
        xn32 = [ph1.enter_context(nc.sbuf_tensor(f"xn32{i}", [128, DM], F32)) for i in range(2)]
        x2b = [ph1.enter_context(nc.sbuf_tensor(f"x2b{i}", [128, DM], BF16)) for i in range(2)]
        dxs = [d_ld0, d_ld1]
        dsp = [d_sp0, d_sp1]
        for i in range(16):
            sl = i % 2
            kb.wait("q", dsp[sl])
            kb.dma("q", lambda q, i=i, sl=sl: q.dma_start(
                out=xt[sl][:, :], in_=x_in[i * 128:(i + 1) * 128, :]), inc=dxs[sl])
            kb.wait("v", dxs[sl])
            kb.wait("v", dsp[sl])
            kb.wait("s", dxs[sl])
            ln_tile(lambda sl=sl: xt[sl][:, :], 128, DM, lambda sl=sl: xnb[sl][:, :],
                    out2_fn=lambda sl=sl: x2b[sl][:, :], absmax_col=mx16[:, i:i + 1],
                    out32_fn=lambda sl=sl: xn32[sl][:, :])
            kb.wait("g", s_v)
            kb.dma("g", lambda g, i=i, sl=sl: g.dma_start(
                out=d_xn[i * 128:(i + 1) * 128, :], in_=xnb[sl][:, :]), inc=dsp[sl])
            kb.dma("g", lambda g, i=i, sl=sl: g.dma_start(
                out=d_xnf[i * 128:(i + 1) * 128, :], in_=xn32[sl][:, :]), inc=dsp[sl])
            kb.dma("g", lambda g, i=i, sl=sl: g.dma_start(
                out=d_x2[i * 128:(i + 1) * 128, :], in_=x2b[sl][:, :]), inc=dsp[sl])
        kb.vsync(s_v, s_s)
        kb.op("v", lambda v: v.tensor_reduce(maxxn[:, :], mx16[:, :], AX.X, ALU.max),
              inc=s_v)
        part_max_to_val(maxxn, 0)
        kb.wait("s", s_v)
        kb.op("s", lambda s: s.copy(payl[0:1, 0:1], vals[0:1, 0:1]), inc=s_s)
        kb.wait("g", s_s)
        kb.dma("g", lambda g: g.dma_start(out=cA_in.ap(), in_=payl[:, :]), inc=d_cc)
        kb.wait("g", d_cc)
        kb.op("g", lambda g: g.collective_compute(
            "AllGather", ALU.bypass, replica_groups=[list(range(8))],
            ins=[cA_in.ap().opt()], outs=[cA_out.ap().opt()]), inc=c_A, n=1)

        ph1.close()

        ph2 = ExitStack()
        x2T = [ph2.enter_context(nc.sbuf_tensor(f"x2T{i}", [128, KD * BLK], BF16))
               for i in range(2)]
        xbcb = ph2.enter_context(nc.sbuf_tensor("xbcb", [128, 18 * 515], BF16))
        zst = [ph2.enter_context(nc.sbuf_tensor(f"zst{i}", [128, BLK], BF16)) for i in range(2)]
        cvt = [ph2.enter_context(nc.sbuf_tensor(f"cvt{i}", [128, BLK], BF16)) for i in range(2)]
        cvo = [ph2.enter_context(nc.sbuf_tensor(f"cvo{i}", [128, BLK], BF16)) for i in range(2)]
        dtr = ph2.enter_context(nc.sbuf_tensor("dtr", [32, BLK], F32))
        tda = ph2.enter_context(nc.sbuf_tensor("tda", [32, BLK], F32))
        tdb = ph2.enter_context(nc.sbuf_tensor("tdb", [32, BLK], F32))
        dtv = ph2.enter_context(nc.sbuf_tensor("dtv", [32, BLK], F32))
        avb = ph2.enter_context(nc.sbuf_tensor("avb", [32, BLK], F32))
        xtok = [ph2.enter_context(nc.sbuf_tensor(f"xtok{i}", [128, DI], BF16)) for i in range(2)]
        xsc = ph2.enter_context(nc.sbuf_tensor("xsc", [128, DI], BF16))
        htmp = ph2.enter_context(nc.sbuf_tensor("htmp", [128, DI], BF16))
        esr = ph2.enter_context(nc.sbuf_tensor("esr", [1, 32], BF16))
        e2h = ph2.enter_context(nc.sbuf_tensor("e2h", [32, 128], F32))
        e2hb = ph2.enter_context(nc.sbuf_tensor("e2hb", [32, 128], F32))
        btf = ph2.enter_context(nc.sbuf_tensor("btf", [128, 128], F32))

        def ap3(tile, off, d1, d2):
            a = tile.ap()
            return bap(a.tensor, [a.ap[0], d1, d2], offset=a.offset + off)

        def emit_march(ci):
            c0 = ci * CHK
            tb = ci * 32
            kb.wait("t", s_v)
            kb.wait("t", s_s)
            kb.op("t", lambda t: t.transpose(pH[:, 256:288], ldms[:, c0:c0 + CHK],
                                             idn_f[0:32, 0:32]))
            kb.op("t", lambda t: t.transpose(pH[:, 288:320], srel[:, c0:c0 + CHK],
                                             idn_f[0:32, 0:32]), inc=s_t)
            kb.wait("s", s_t)
            kb.op("s", lambda s: s.copy(tcolA[:, tb:tb + 32], pH[:, 256:288]))
            kb.op("s", lambda s: s.activation(estA[:, tb:tb + 32], pH[:, 288:320], AF.Exp),
                  inc=s_s)
            kb.op("t", lambda t: t.transpose(pH[0:1, 128:160], srel[:, c0 + 127:c0 + 128],
                                             idn_f[0:32, 0:32]), inc=s_t)
            kb.wait("s", s_t)
            kb.op("s", lambda s: s.activation(esr[:, :], pH[0:1, 128:160], AF.Exp), inc=s_s)
            kb.wait("t", s_s)
            kb.op("t", lambda t: t.matmul(pH[:, 160:192], onesb[:, :], esr[:, :],
                                          start=True, stop=True), inc=s_t)
            kb.wait("s", s_t)
            kb.op("s", lambda s: s.copy(ebcA[:, tb:tb + 32], pH[:, 160:192]), inc=s_s)
            kb.op("v", lambda v: v.tensor_scalar_add(e2h[:, :], ldms[:, c0:c0 + CHK],
                                                     srel[:, c0 + 127:c0 + 128]), inc=s_v)
            kb.wait("s", s_v)
            kb.op("s", lambda s: s.activation(e2hb[:, :], e2h[:, :], AF.Exp))
            kb.op("s", lambda s, c0=c0: s.copy(btf[:, :], BT[:, c0:c0 + CHK]), inc=s_s)
            kb.wait("t", s_s)
            kb.op("t", lambda t: t.transpose(pH[:, 224:256], e2hb[:, :], idn_f[0:32, 0:32]),
                  inc=s_t)
            kb.op("t", lambda t: t.transpose(pH[:, 0:128], btf[:, :], idn_f[:, :]), inc=s_t)
            kb.wait("s", s_t)
            kb.op("s", lambda s: s.copy(e2tA[:, tb:tb + 32], pH[:, 224:256]))
            kb.op("s", lambda s: s.copy(btokA[:, ci * 128:(ci + 1) * 128], pH[:, 0:128]),
                  inc=s_s)
            sl = ci % 2
            kb.wait("g", s_s)
            kb.op("g", lambda g, sl=sl: g.tensor_tensor(
                xsc[:, :], xtok[sl][:, :], ap3(e2tA, tb, [1, 32], [0, 64]), ALU.mult),
                inc=s_g)
            kb.wait("t", s_g)
            kb.wait("t", evW)
            for q in range(4):
                kb.op("t", lambda t, ci=ci, q=q: t.matmul(
                    pW[:, q * 512:(q + 1) * 512], btokA[:, ci * 128:(ci + 1) * 128],
                    xsc[:, q * 512:(q + 1) * 512],
                    start=True, stop=True), inc=(s_t if q == 3 else None))
            t_dh = s_t.n
            kb.op("g", lambda g, tb=tb: g.tensor_tensor(
                htmp[:, :], hst[:, :], ap3(ebcA, tb, [1, 32], [0, 64]), ALU.mult), inc=s_g)
            kb.wait("v", s_g)
            kb.wait("v", s_t, t_dh)
            kb.op("v", lambda v: v.tensor_add(hst[:, :], htmp[:, :], pW[:, :]), inc=evW)

        def emit_x2T_reload(b):
            t0 = b * BLK
            kb.wait("q", d_sp0)
            kb.wait("q", d_sp1)
            kb.wait("q", s_t)
            for k in range(KD):
                kb.dma("q", lambda q, k=k, t0=t0, b=b: q.dma_start(
                    out=x2T[b % 2][:, k * BLK:(k + 1) * BLK],
                    in_=d_x2[t0:t0 + BLK, k * 128:(k + 1) * 128], transpose=True),
                    inc=d_x2t[b % 2])

        emit_x2T_reload(0)
        for b in range(NBLK):
            t0 = b * BLK
            if b + 1 < NBLK:
                emit_x2T_reload(b + 1)
            kb.wait("t", d_x2t[b % 2])
            if b == 0:
                kb.wait("t", s_v, v_halo)
                kb.wait("v", d_sp0)
                kb.wait("v", d_sp1)
                kb.wait("v", s_v, v_halo)
                for i in range(18):
                    kb.op("v", lambda v, i=i: v.tensor_copy(
                        xbcb[:, i * 515:i * 515 + 3], hal[:, i * 3:(i + 1) * 3]), inc=s_v)
            for j in range(35):
                ps, evs = (pA, evA) if j % 2 == 0 else (pB, evB)
                kb.wait("t", evs)
                for k in range(KD):
                    kb.op("t", lambda t, ps=ps, j=j, k=k, b=b, f=(k == 0), l=(k == KD - 1):
                          t.matmul(ps[:, :], W[:, k * DIP + j * 128:k * DIP + (j + 1) * 128],
                                   x2T[b % 2][:, k * BLK:(k + 1) * BLK], start=f, stop=l),
                          inc=(s_t if k == KD - 1 else None))
                if j < 16:
                    sl = j % 2
                    kb.wait("s", s_t)
                    kb.wait("s", dsp[sl])
                    kb.op("s", lambda s, ps=ps, j=j, sl=sl: s.activation(
                        zst[sl][:, :], ps[:, :], AF.Silu, bias=biasj[:, j:j + 1]), inc=evs)
                    kb.wait("g", evs)
                    kb.dma("g", lambda g, j=j, sl=sl, t0=t0: g.dma_start(
                        out=d_z[j * 128:(j + 1) * 128, t0:t0 + BLK], in_=zst[sl][:, :]),
                        inc=dsp[sl])
                elif j < 34:
                    i = j - 16
                    kb.wait("s", s_t)
                    kb.op("s", lambda s, ps=ps, j=j, i=i: s.activation(
                        xbcb[:, i * 515 + 3:i * 515 + 515], ps[:, :], AF.Identity,
                        bias=biasj[:, j:j + 1]), inc=evs)
                else:
                    kb.wait("s", s_t)
                    kb.op("s", lambda s, ps=ps: s.activation(
                        dtr[:, :], ps[0:32, :], AF.Identity, bias=b34[:, :]), inc=evs)
            kb.wait("v", evA)
            kb.wait("v", evB)
            for i in range(18):
                base = i * 515
                wof = i * 4
                sl = i % 2
                kb.wait("v", dsp[sl])
                kb.op("v", lambda v, base=base, wof=wof, sl=sl: v.tensor_scalar_mul(
                    cvt[sl][:, :], xbcb[:, base + 0:base + BLK], cwgt[:, wof:wof + 1]))
                for kk in range(1, 4):
                    src, dst = (cvt[sl], cvo[sl]) if kk % 2 == 1 else (cvo[sl], cvt[sl])
                    kb.op("v", lambda v, base=base, wof=wof, kk=kk, src=src, dst=dst:
                          v.scalar_tensor_tensor(
                              dst[:, :], xbcb[:, base + kk:base + kk + BLK],
                              cwgt[:, wof + kk:wof + kk + 1], src[:, :],
                              ALU.mult, ALU.add), inc=(s_v if kk == 3 else None))
                acc = cvo[sl] if 3 % 2 == 1 else cvt[sl]
                kb.wait("s", s_v)
                if i < 16:
                    kb.wait("s", dsp[sl])
                    kb.op("s", lambda s, acc=acc, i=i, sl=sl: s.activation(
                        cvo[sl][:, :] if acc is cvt[sl] else cvt[sl][:, :],
                        acc[:, :], AF.Silu, bias=cbia[:, i:i + 1]), inc=s_s)
                    dst = cvo[sl] if acc is cvt[sl] else cvt[sl]
                    kb.wait("g", s_s)
                    kb.dma("g", lambda g, i=i, t0=t0, dst=dst: g.dma_start(
                        out=d_xc[i * 128:(i + 1) * 128, t0:t0 + BLK], in_=dst[:, :]),
                        inc=dsp[sl])
                else:
                    tgt = BT if i == 16 else CTt
                    kb.op("s", lambda s, acc=acc, i=i, tgt=tgt, t0=t0: s.activation(
                        tgt[:, t0:t0 + BLK], acc[:, :], AF.Silu, bias=cbia[:, i:i + 1]),
                        inc=s_s)
                kb.wait("v", s_s)
                kb.op("v", lambda v, base=base: v.tensor_copy(
                    xbcb[:, base:base + 3], xbcb[:, base + BLK:base + BLK + 3]), inc=s_v)
            kb.op("s", lambda s: s.activation(tda[:, :], dtr[:, :], AF.Abs))
            kb.op("s", lambda s: s.activation(tdb[:, :], tda[:, :], AF.Exp, scale=-1.0))
            kb.op("s", lambda s: s.activation(tda[:, :], tdb[:, :], AF.Ln, bias=1.0), inc=s_s)
            kb.wait("v", s_s)
            kb.op("v", lambda v: v.tensor_scalar_max(tdb[:, :], dtr[:, :], 0.0))
            kb.op("v", lambda v: v.tensor_add(dtv[:, :], tda[:, :], tdb[:, :]))
            kb.op("v", lambda v: v.tensor_scalar_mul(avb[:, :], dtv[:, :], nac[:, :]), inc=s_v)
            kb.wait("s", s_v)
            kb.op("s", lambda s: s.activation(tda[:, :], dtv[:, :], AF.Ln), inc=s_s)
            kb.op("v", lambda v, t0=t0: v.tensor_tensor_scan(
                srel[:, t0:t0 + BLK], mres[:, :], avb[:, :], 0.0, ALU.mult, ALU.add), inc=s_v)
            kb.wait("v", s_s)
            kb.op("v", lambda v, t0=t0: v.tensor_sub(ldms[:, t0:t0 + BLK], tda[:, :],
                                                     srel[:, t0:t0 + BLK]), inc=s_v)
            sr_ap = srel.ap()
            kb.op("v", lambda v, t0=t0, b=b: v.tensor_reduce(
                dtsc[:, b:b + 1],
                bap(sr_ap.tensor, [sr_ap.ap[0], [CHK, 4], [1, 1]],
                    offset=sr_ap.offset + t0 + 127),
                AX.XY, ALU.add), inc=s_v)
            kb.wait("g", s_v)
            kb.dma("g", lambda g, t0=t0: g.dma_start(out=d_sr[:, t0:t0 + BLK],
                                                     in_=srel[:, t0:t0 + BLK]), inc=d_sp0)
            for c in range(4):
                ci = b * 4 + c
                kb.dma("g", lambda g, ci=ci: g.dma_start(
                    out=bap(d_srf.ap().tensor, [[128, 32], [1, CHK]], offset=ci * 4096),
                    in_=srel[:, ci * CHK:(ci + 1) * CHK]), inc=d_sp0)
            if b > 0:
                for c in range(4):
                    ci = (b - 1) * 4 + c
                    sl = ci % 2
                    kb.wait("q", d_sp0)
                    kb.wait("q", d_sp1)
                    kb.wait("q", s_g)
                    kb.dma("q", lambda q, ci=ci, sl=sl: q.dma_start(
                        out=xtok[sl][:, :], in_=d_xc[:, ci * CHK:(ci + 1) * CHK],
                        transpose=True), inc=d_xtk[sl])
                    kb.wait("g", d_xtk[sl])
                    emit_march(ci)
        for c in range(4):
            ci = 12 + c
            sl = ci % 2
            kb.wait("q", d_sp0)
            kb.wait("q", d_sp1)
            kb.wait("q", s_g)
            kb.dma("q", lambda q, ci=ci, sl=sl: q.dma_start(
                out=xtok[sl][:, :], in_=d_xc[:, ci * CHK:(ci + 1) * CHK], transpose=True),
                inc=d_xtk[sl])
            kb.wait("g", d_xtk[sl])
            emit_march(ci)

        kb.op("v", lambda v: v.tensor_reduce(elog[:, :], dtsc[:, :], AX.X, ALU.add),
              inc=s_v)
        kb.wait("g", evW)
        kb.dma("g", lambda g: g.dma_start(out=cD_in.ap(), in_=hst[:, :]), inc=d_cc)
        kb.wait("t", s_v)
        kb.op("t", lambda t: t.transpose(pH[0:1, 128:160], elog[:, :], idn_f[0:32, 0:32]),
              inc=s_t)
        kb.wait("v", s_t)
        kb.wait("v", d_cc)
        kb.op("v", lambda v: v.tensor_copy(payl[0:1, 0:32], pH[0:1, 128:160]), inc=s_v)
        kb.wait("g", s_v)
        kb.dma("g", lambda g: g.dma_start(out=cB_in.ap(), in_=payl[:, :]), inc=d_cc)
        kb.wait("g", d_cc)
        kb.op("g", lambda g: g.collective_compute(
            "AllGather", ALU.bypass, replica_groups=[list(range(8))],
            ins=[cB_in.ap().opt()], outs=[cB_out.ap().opt()]), inc=c_B, n=1)
        kb.op("g", lambda g: g.collective_compute(
            "AllGather", ALU.bypass, replica_groups=[[0, 1, 2, 3], [4, 5, 6, 7]],
            ins=[cD_in.ap().opt()], outs=[cD_out.ap().opt()]), inc=c_D, n=1)
        ph2.close()
        phW.close()

        kb.barrier(s_bar, dma_sems=(d_sp0, d_sp1, d_tr, d_w, d_cc, d_x))

        ph3 = ExitStack()
        xqT = ph3.enter_context(nc.sbuf_tensor("xqT", [128, KD * T], BF16))
        w1q = ph3.enter_context(nc.sbuf_tensor("w1q", [128, KD * FFN], BF16))
        w1st = [ph3.enter_context(nc.sbuf_tensor(f"w1st{i}", [128, FFN], F32)) for i in range(2)]
        qa = ph3.enter_context(nc.sbuf_tensor("qa", [128, DM], F32))
        xft = [ph3.enter_context(nc.sbuf_tensor(f"xft{i}", [128, DM], F32)) for i in range(2)]
        xqt = [ph3.enter_context(nc.sbuf_tensor(f"xqt{i}", [128, DM], BF16)) for i in range(2)]
        hso = [ph3.enter_context(nc.sbuf_tensor(f"hso{i}", [128, BLK], F32)) for i in range(2)]
        mhacc = ph3.enter_context(nc.sbuf_tensor("mhacc", [128, 128], F32))
        gB8 = ph3.enter_context(nc.sbuf_tensor("gB8", [8, 32], F32))
        gBb = ph3.enter_context(nc.sbuf_tensor("gBb", [8, 32], BF16))
        coefs = ph3.enter_context(nc.sbuf_tensor("coefs", [8, 32], F32))
        crow = ph3.enter_context(nc.sbuf_tensor("crow", [1, 32], BF16))
        cbc = ph3.enter_context(nc.sbuf_tensor("cbc", [128, 32], BF16))
        dhg = ph3.enter_context(nc.sbuf_tensor("dhg", [128, DI], BF16))

        wm8 = ph3.enter_context(nc.sbuf_tensor("wm8", [128, 8], F32))
        for k in range(KD):
            sl = k % 2
            kb.wait("g", s_v)
            kb.dma("g", lambda g, k=k, sl=sl: g.dma_start(
                out=w1st[sl][:, :], in_=w1_in[k * 128:(k + 1) * 128, :]), inc=d_w)
            kb.wait("v", d_w)
            kb.op("v", lambda v, sl=sl, k=k: v.reduce_max(wm8[:, k:k + 1], w1st[sl][:, :],
                                                          AX.X, apply_absolute_value=True),
                  inc=s_v)
        kb.wait("g", c_A, 1)
        gA8 = ph3.enter_context(nc.sbuf_tensor("gA8", [1, 8], F32))
        kb.dma("g", lambda g: g.dma_start(
            out=gA8[:, :], in_=bap(cA_out.ap().tensor, [[1, 1], [16 * 128, 8]]),
            allow_slow_non_contiguous=True), inc=d_cc)
        kb.wait("v", d_cc)
        kb.vsync(s_v, s_s)
        kb.op("v", lambda v: v.tensor_reduce(vals[0:1, 2:3], gA8[:, :], AX.X, ALU.max))
        kb.op("v", lambda v: v.tensor_reduce(stats[:, 9:10], wm8[:, :], AX.X, ALU.max),
              inc=s_v)
        kb.wait("t", s_v)
        kb.op("t", lambda t: t.transpose(pH[0:1, 0:128], stats[:, 9:10], idn_f[:, :]), inc=s_t)
        kb.vsync(s_v, s_s)
        kb.wait("v", s_t)
        kb.op("v", lambda v: v.tensor_reduce(vals[0:1, 1:2], pH[0:1, 0:128], AX.X, ALU.max))
        kb.op("v", lambda v: v.tensor_scalar_max(vals[0:1, 2:3], vals[0:1, 2:3], 1e-6),
              inc=s_v)
        kb.vsync(s_v, s_s)
        kb.op("v", lambda v: v.tensor_scalar_max(vals[0:1, 1:2], vals[0:1, 1:2], 1e-6))
        kb.op("v", lambda v: v.tensor_scalar_mul(vals[0:1, 3:4], vals[0:1, 2:3], R127),
              inc=s_v)
        kb.vsync(s_v, s_s)
        kb.op("v", lambda v: v.tensor_scalar_mul(vals[0:1, 5:6], vals[0:1, 1:2], R127))
        kb.op("v", lambda v: v.reciprocal(vals[0:1, 4:5], vals[0:1, 3:4]), inc=s_v)
        kb.vsync(s_v, s_s)
        kb.op("v", lambda v: v.reciprocal(vals[0:1, 6:7], vals[0:1, 5:6]))
        kb.op("v", lambda v: v.tensor_mul(vals[0:1, 7:8], vals[0:1, 3:4], vals[0:1, 5:6]),
              inc=s_v)
        kb.vsync(s_v, s_s)
        kb.wait("g", s_v)
        kb.dma("g", lambda g: g.dma_start(out=d_val.ap(), in_=vals[:, :]), inc=d_cc)
        kb.wait("g", d_cc)
        for slot, ci in [(4, 0), (6, 1), (7, 2)]:
            kb.dma("g", lambda g, slot=slot, ci=ci: g.dma_start(
                out=cols[:, ci:ci + 1],
                in_=bap(d_val.ap().tensor, [[0, 128], [1, 1]], offset=slot)), inc=d_cc)
        kb.wait("v", d_cc)
        kb.op("v", lambda v: v.tensor_scalar_mul(b1s[:, :], b1sb[:, :], cols[:, 2:3]), inc=s_v)
        v_sc1 = s_v.n
        for i in range(16):
            sl = i % 2
            kb.wait("q", dsp[sl])
            kb.dma("q", lambda q, i=i, sl=sl: q.dma_start(
                out=xft[sl][:, :], in_=d_xnf[i * 128:(i + 1) * 128, :]), inc=dxs[sl])
            kb.wait("s", dxs[sl])
            kb.wait("s", s_v)
            kb.op("s", lambda s, sl=sl: s.activation(
                qa[:, :], xft[sl][:, :], AF.Copy, bias=MAGIC, scale=cols[:, 0:1]), inc=s_s)
            kb.wait("v", s_s)
            kb.wait("v", dsp[sl])
            kb.op("v", lambda v, sl=sl: v.tensor_scalar(
                xqt[sl][:, :], qa[:, :], MAGIC, None, ALU.subtract), inc=s_v)
            kb.wait("g", s_v)
            kb.dma("g", lambda g, i=i, sl=sl: g.dma_start(
                out=d_xq[i * 128:(i + 1) * 128, :], in_=xqt[sl][:, :]), inc=dsp[sl])
        kb.wait("q", dsp[0])
        kb.wait("q", dsp[1])
        for k in range(KD):
            kb.dma("q", lambda q, k=k: q.dma_start(
                out=xqT[:, k * T:(k + 1) * T], in_=d_xq[:, k * 128:(k + 1) * 128],
                transpose=True), inc=d_tr)
        for k in range(KD):
            sl = k % 2
            kb.wait("g", s_v)
            kb.dma("g", lambda g, k=k, sl=sl: g.dma_start(
                out=w1st[sl][:, :], in_=w1_in[k * 128:(k + 1) * 128, :]), inc=d_w)
            kb.wait("v", d_w)
            kb.op("v", lambda v, sl=sl: v.tensor_scalar(
                w1st[1 - sl][:, :], w1st[sl][:, :], cols[:, 1:2], MAGIC, ALU.mult, ALU.add))
            kb.op("v", lambda v, k=k, sl=sl: v.tensor_scalar(
                w1q[:, k * FFN:(k + 1) * FFN], w1st[1 - sl][:, :], MAGIC, None,
                ALU.subtract), inc=s_v)
        v_w1q = s_v.n
        kb.wait("t", d_tr)
        kb.wait("s", d_tr)
        for b in range(NBLK):
            t0 = b * BLK
            for j in range(32):
                ps, evs = (pA, evA) if j % 2 == 0 else (pB, evB)
                kb.wait("t", evs)
                if b == 0 and j == 0:
                    kb.wait("t", d_tr)
                    kb.wait("t", s_v)
                for k in range(KD):
                    kb.op("t", lambda t, ps=ps, j=j, k=k, t0=t0, f=(k == 0), l=(k == KD - 1):
                          t.matmul(ps[:, :],
                                   w1q[:, k * FFN + j * 128:k * FFN + (j + 1) * 128],
                                   xqT[:, k * T + t0:k * T + t0 + BLK], start=f, stop=l),
                          inc=(s_t if k == KD - 1 else None))
                sl = j % 2
                kb.wait("s", s_t)
                kb.wait("s", dsp[sl])
                kb.wait("s", s_v)
                kb.op("s", lambda s, ps=ps, j=j, sl=sl: s.activation(
                    hso[sl][:, :], ps[:, :], AF.Gelu, bias=b1s[:, j:j + 1],
                    scale=cols[:, 2:3]), inc=evs)
                kb.wait("v", evs)
                kb.op("v", lambda e, sl=sl, j=j, b=b: e.reduce_max(
                    mhacc[:, b * 32 + j:b * 32 + j + 1], hso[sl][:, :], AX.X,
                    apply_absolute_value=True), inc=s_v)
                kb.wait("g", evs)
                kb.dma("g", lambda g, j=j, t0=t0, sl=sl: g.dma_start(
                    out=d_h[j * 128:(j + 1) * 128, t0:t0 + BLK], in_=hso[sl][:, :]),
                    inc=dsp[sl])
        kb.vsync(s_v, s_s)
        kb.op("v", lambda v: v.tensor_reduce(maxh[:, :], mhacc[:, :], AX.X, ALU.max),
              inc=s_v)
        kb.wait("t", s_v)
        kb.op("t", lambda t: t.transpose(pH[0:1, 0:128], maxh[:, :], idn_f[:, :]), inc=s_t)
        kb.wait("v", s_t)
        kb.op("v", lambda v: v.tensor_reduce(vals[0:1, 9:10], pH[0:1, 0:128], AX.X,
                                             ALU.max), inc=s_v)
        kb.vsync(s_v, s_s)
        kb.wait("s", d_cc)
        kb.wait("s", s_v)
        kb.op("s", lambda s: s.copy(payl[0:1, 0:1], vals[0:1, 9:10]))
        kb.op("s", lambda s: s.memzero(payl[0:1, 1:32]), inc=s_s)
        kb.wait("g", s_s)
        kb.dma("g", lambda g: g.dma_start(out=cC_in.ap(), in_=payl[:, :]), inc=d_cc)
        kb.wait("g", d_cc)
        kb.op("g", lambda g: g.collective_compute(
            "AllGather", ALU.bypass, replica_groups=[list(range(8))],
            ins=[cC_in.ap().opt()], outs=[cC_out.ap().opt()]), inc=c_C, n=1)

        kb.wait("g", c_B, 1)
        kb.dma("g", lambda g: g.dma_start(
            out=gB8[:, :], in_=bap(cB_out.ap().tensor, [[16 * 128, 8], [1, 32]])), inc=d_cc)
        kb.wait("v", d_cc)
        kb.op("v", lambda v: v.tensor_copy(gBb[:, :], gB8[:, :]), inc=s_v)
        kb.wait("t", s_v)
        kb.op("t", lambda t: t.matmul(pH[0:8, 0:32], mmatb[:, :], gBb[:, :],
                                      start=True, stop=True), inc=s_t)
        kb.wait("s", s_t)
        kb.op("s", lambda s: s.activation(coefs[:, :], pH[0:8, 0:32], AF.Exp), inc=s_s)
        kb.wait("v", s_s)
        kb.op("v", lambda v: v.tensor_scalar_mul(coefs[:, :], coefs[:, :], mmaskc[:, :]),
              inc=s_v)
        kb.wait("g", s_v)
        kb.dma("g", lambda g: g.dma_start(out=d_coef.ap(), in_=coefs[:, :]), inc=d_cc)
        kb.wait("g", c_D, 1)
        kb.op("g", lambda g: g.memset(sq_scr[:, 0:DI], 0.0), inc=s_g)
        kb.wait("v", s_g)
        for r in range(4):
            kb.wait("g", d_cc)
            kb.dma("g", lambda g, r=r: g.dma_start(
                out=crow[:, :], in_=d_coef[r:r + 1, :]), inc=d_cc)
            kb.wait("t", d_cc)
            kb.op("t", lambda t: t.matmul(pH[:, 192:224], onesb[:, :], crow[:, :],
                                          start=True, stop=True), inc=s_t)
            kb.wait("s", s_t)
            kb.op("s", lambda s: s.copy(cbc[:, :], pH[:, 192:224]), inc=s_s)
            kb.wait("g", s_s)
            kb.dma("g", lambda g, r=r: g.dma_start(
                out=dhg[:, :], in_=cD_out[r * 128:(r + 1) * 128, :]), inc=d_cc)
            kb.wait("v", d_cc)
            kb.wait("v", s_s)
            kb.op("v", lambda v: v.tensor_tensor(
                dhg[:, :], dhg[:, :],
                bap(cbc.ap().tensor, [cbc.ap().ap[0], [1, 32], [0, 64]],
                    offset=cbc.ap().offset), ALU.mult))
            kb.op("v", lambda v: v.tensor_add(sq_scr[:, 0:DI], sq_scr[:, 0:DI], dhg[:, :]), inc=s_v)
            kb.wait("g", s_v)
        kb.op("v", lambda v: v.tensor_copy(hst[:, :], sq_scr[:, 0:DI]), inc=s_v)
        ph3.close()

        kb.barrier(s_bar)

        ph4 = ExitStack()
        sbc = [ph4.enter_context(nc.sbuf_tensor(f"sbc{i}", [128, 4096], F32)) for i in range(2)]
        dbuf = ph4.enter_context(nc.sbuf_tensor("dbuf", [128, 4096], F32))
        gsb = ph4.enter_context(nc.sbuf_tensor("gsb", [128, 128], BF16))
        lbuf = ph4.enter_context(nc.sbuf_tensor("lbuf", [128, 4096], BF16))
        wbuf = [ph4.enter_context(nc.sbuf_tensor(f"wbuf{i}", [128, 4096], BF16))
                for i in range(2)]
        ybuf = ph4.enter_context(nc.sbuf_tensor("ybuf", [128, DI], F32))
        ygb = ph4.enter_context(nc.sbuf_tensor("ygb", [128, DI], F32))
        xtok4 = [ph4.enter_context(nc.sbuf_tensor(f"xtok4{i}", [128, DI], BF16)) for i in range(2)]
        ztok = [ph4.enter_context(nc.sbuf_tensor(f"ztok{i}", [128, DI], BF16)) for i in range(2)]
        xdb = ph4.enter_context(nc.sbuf_tensor("xdb", [128, DI], BF16))
        xsc4 = ph4.enter_context(nc.sbuf_tensor("xsc4", [128, DI], BF16))
        htmp4 = ph4.enter_context(nc.sbuf_tensor("htmp4", [128, DI], BF16))
        yrst = [ph4.enter_context(nc.sbuf_tensor(f"yrst{i}", [128, DI], BF16)) for i in range(2)]
        ssq = ph4.enter_context(nc.sbuf_tensor("ssq", [128, 4], F32))


        kb.wait("q", d_sp0)
        kb.wait("q", d_sp1)

        def p4_prefetch(ci):
            sl = ci % 2
            c0 = ci * CHK
            kb.wait("q", s_g)
            kb.wait("q", s_v)
            kb.dma("q", lambda q, ci=ci, sl=sl: q.dma_start(
                out=sbc[sl][:, :],
                in_=bap(d_srf.ap().tensor, [[0, 128], [1, 4096]], offset=ci * 4096)),
                inc=d_xtk[sl])
            kb.dma("q", lambda q, c0=c0, sl=sl: q.dma_start(
                out=xtok4[sl][:, :], in_=d_xc[:, c0:c0 + CHK], transpose=True),
                inc=d_xtk[sl])
            kb.dma("q", lambda q, c0=c0, sl=sl: q.dma_start(
                out=ztok[sl][:, :], in_=d_z[:, c0:c0 + CHK], transpose=True),
                inc=d_xtk[sl])

        def p4_head(ci):
            sl = ci % 2
            tb = ci * 32
            c0 = ci * CHK
            kb.wait("t", evB)
            kb.op("t", lambda t, c0=c0: t.matmul(pB[:, 0:128], BT[:, c0:c0 + CHK],
                                                 CTt[:, c0:c0 + CHK], start=True, stop=True),
                  inc=s_t)
            kb.wait("s", s_t)
            kb.wait("s", s_v)
            kb.op("s", lambda s: s.copy(gsb[:, :], pB[:, 0:128]), inc=evB)
            kb.wait("v", d_xtk[sl])
            HB = 4096
            NHH = 1
            for hh in range(NHH):
                o = hh * HB
                NHD = 32 // NHH
                kb.op("v", lambda v, sl=sl, tb=tb, o=o, NHD=NHD: v.tensor_tensor(
                    bap(dbuf.ap().tensor, [dbuf.ap().ap[0], [CHK, NHD], [1, CHK]],
                        offset=dbuf.ap().offset + o),
                    bap(sbc[sl].ap().tensor, [sbc[sl].ap().ap[0], [CHK, NHD], [1, CHK]],
                        offset=sbc[sl].ap().offset + o),
                    ap3(tcolA, tb + hh * NHD, [1, NHD], [0, CHK]), ALU.add), inc=s_v)
                kb.wait("g", s_v)
                kb.op("g", lambda g, o=o, NHD=NHD: g.affine_select(
                    lbuf[:, o:o + HB], dbuf[:, o:o + HB], [[0, NHD], [1, CHK]],
                    ALU.is_ge, NEG, base=0, channel_multiplier=-1), inc=s_g)
                kb.wait("s", s_g)
                kb.op("s", lambda s, o=o: s.activation(lbuf[:, o:o + HB], lbuf[:, o:o + HB],
                                                       AF.Exp), inc=s_s)
                kb.wait("v", evB)
                kb.wait("v", s_s)
                kb.op("v", lambda v, sl=sl, o=o, NHD=NHD: v.tensor_tensor(
                    wbuf[sl][:, o:o + HB], lbuf[:, o:o + HB],
                    bap(gsb.ap().tensor, [gsb.ap().ap[0], [0, NHD], [1, CHK]],
                        offset=gsb.ap().offset), ALU.mult), inc=s_v)
            wmark[ci % 2] = s_v.n

        wmark = {0: 0, 1: 0}
        p4_prefetch(0)
        p4_head(0)
        for ci in range(NCHK):
            sl = ci % 2
            c0 = ci * CHK
            tb = ci * 32
            if ci + 1 < NCHK:
                p4_prefetch(ci + 1)
            kb.wait("t", evW)
            kb.wait("t", evA)
            kb.wait("t", evS)
            for q in range(4):
                kb.op("t", lambda t, c0=c0, q=q: t.matmul(
                    pW[:, q * 512:(q + 1) * 512], CTt[:, c0:c0 + CHK],
                    hst[:, q * 512:(q + 1) * 512], start=True, stop=True),
                    inc=(s_t if q == 3 else None))
            kb.wait("g", d_xtk[sl])
            kb.op("g", lambda g, sl=sl: g.tensor_tensor(
                xdb[:, :], xtok4[sl][:, :],
                bap(DT32.ap().tensor, [DT32.ap().ap[0], [1, 32], [0, 64]],
                    offset=DT32.ap().offset), ALU.mult), inc=s_g)
            g_xd = s_g.n
            kb.op("g", lambda g, sl=sl, tb=tb: g.tensor_tensor(
                xsc4[:, :], xtok4[sl][:, :], ap3(e2tA, tb, [1, 32], [0, 64]), ALU.mult),
                inc=s_g)
            kb.wait("g", s_t)
            kb.op("g", lambda g, tb=tb: g.tensor_tensor(
                htmp4[:, :], hst[:, :], ap3(ebcA, tb, [1, 32], [0, 64]), ALU.mult), inc=s_g)
            kb.wait("v", s_t)
            kb.op("v", lambda v, tb=tb: v.tensor_tensor(
                ybuf[:, :], pW[:, :], ap3(estA, tb, [1, 32], [0, 64]), ALU.mult), inc=evW)
            kb.wait("t", evW)
            kb.wait("t", s_v, wmark[sl])
            kb.wait("t", s_g, g_xd)
            for h in range(NH):
                kb.op("t", lambda t, h=h, sl=sl: t.matmul(
                    pW[:, h * 64:(h + 1) * 64], wbuf[sl][:, h * 128:(h + 1) * 128],
                    xtok4[sl][:, h * 64:(h + 1) * 64], start=(h % 8 == 0), stop=False))
            for q in range(4):
                kb.op("t", lambda t, q=q: t.matmul(
                    pW[:, q * 512:(q + 1) * 512], idn_b[:, :], xdb[:, q * 512:(q + 1) * 512],
                    start=False, stop=True), inc=(s_t if q == 3 else None))
            kb.wait("t", evA)
            kb.wait("t", evS)
            kb.wait("t", s_g)
            for q in range(2):
                ps_q = pA if q == 0 else pS
                kb.op("t", lambda t, ci=ci, q=q, ps_q=ps_q: t.matmul(
                    ps_q[:, :], btokA[:, ci * 128:(ci + 1) * 128],
                    xsc4[:, q * 512:(q + 1) * 512], start=True, stop=True), inc=s_t)
            t_dh01 = s_t.n
            kb.wait("v", s_t)
            kb.op("v", lambda v: v.tensor_add(ybuf[:, :], ybuf[:, :], pW[:, :]), inc=evW)
            kb.wait("v", s_g)
            kb.op("v", lambda v: v.tensor_add(hst[:, 0:512], htmp4[:, 0:512], pA[:, :]),
                  inc=evA)
            kb.op("v", lambda v: v.tensor_add(hst[:, 512:1024], htmp4[:, 512:1024],
                                              pS[:, :]), inc=evS)
            for q in range(2, 4):
                ps_q, ev_q = (pA, evA) if q % 2 == 0 else (pS, evS)
                kb.wait("t", ev_q)
                kb.op("t", lambda t, ci=ci, q=q, ps_q=ps_q: t.matmul(
                    ps_q[:, :], btokA[:, ci * 128:(ci + 1) * 128],
                    xsc4[:, q * 512:(q + 1) * 512], start=True, stop=True), inc=s_t)
                t_dhq = s_t.n
                kb.wait("v", s_t, t_dhq)
                kb.op("v", lambda v, q=q, ps_q=ps_q: v.tensor_add(
                    hst[:, q * 512:(q + 1) * 512], htmp4[:, q * 512:(q + 1) * 512],
                    ps_q[:, :]), inc=ev_q)
            if ci + 1 < NCHK:
                p4_head(ci + 1)
            kb.wait("g", evW)
            kb.op("g", lambda g, sl=sl: g.tensor_mul(ygb[:, :], ybuf[:, :], ztok[sl][:, :]),
                  inc=s_g)
            kb.wait("s", s_g)
            kb.op("s", lambda s: s.activation(sq_scr[:, :], ygb[:, :], AF.Square,
                                              accum_out=ssq[:, 0:1]))
            kb.op("s", lambda s: s.activation(ssq[:, 1:2], ssq[:, 0:1], AF.Ln,
                                              bias=EPS, scale=1.0 / DI))
            kb.op("s", lambda s: s.activation(ssq[:, 2:3], ssq[:, 1:2], AF.Exp,
                                              scale=-0.5), inc=s_s)
            kb.wait("s", dsp[sl])
            kb.op("s", lambda s, sl=sl: s.activation(yrst[sl][:, :], ygb[:, :], AF.Identity,
                                                     scale=ssq[:, 2:3]), inc=s_s)
            kb.wait("g", s_s)
            kb.dma("g", lambda g, c0=c0, sl=sl: g.dma_start(
                out=d_yr[c0:c0 + CHK, :], in_=yrst[sl][:, :]), inc=dsp[sl])
        for k in range(32):
            sl = k % 2
            kb.wait("g", s_v)
            kb.dma("g", lambda g, k=k, sl=sl: g.dma_start(
                out=sbc[sl][:, 0:DM], in_=w2_in[k * 128:(k + 1) * 128, :]), inc=d_w)
            kb.wait("v", d_w)
            kb.op("v", lambda v, sl=sl, k=k: v.reduce_max(wm32[:, k:k + 1], sbc[sl][:, 0:DM],
                                                          AX.X, apply_absolute_value=True),
                  inc=s_v)
        kb.op("v", lambda v: v.nop(), inc=s_v)

        ph4.close()
        phT.close()
        kb.barrier(s_bar, dma_sems=(d_sp0, d_sp1, d_tr, d_w, d_cc))

        sT = sb("sT", [128, 8 * T], BF16)
        ph5 = ExitStack()
        xnT2 = ph5.enter_context(nc.sbuf_tensor("xnT2", [128, KD * T], BF16))
        woq = ph5.enter_context(nc.sbuf_tensor("woq", [128, 16 * DM], BF16))
        yrT = [ph5.enter_context(nc.sbuf_tensor(f"yrT{i}", [128, 16 * BLK], BF16))
               for i in range(2)]
        kb.wait("q", d_sp0)
        kb.wait("q", d_sp1)
        for k in range(KD):
            kb.dma("q", lambda q, k=k: q.dma_start(
                out=xnT2[:, k * T:(k + 1) * T], in_=d_xn[:, k * 128:(k + 1) * 128],
                transpose=True), inc=d_tr)
        for k in range(16):
            kb.dma("g", lambda g, k=k: g.dma_start(
                out=woq[:, k * DM:(k + 1) * DM], in_=wo_in[k * 128:(k + 1) * 128, :]),
                inc=d_w)
        kb.wait("v", d_w)
        for k in range(16):
            kb.op("v", lambda v, k=k: v.tensor_scalar_mul(
                woq[:, k * DM:(k + 1) * DM], woq[:, k * DM:(k + 1) * DM], nwc[:, k:k + 1]),
                inc=s_v)
        for b in range(NBLK):
            t0 = b * BLK
            sl = b % 2
            kb.wait("q", evA)
            kb.wait("q", evB)
            kb.wait("q", dsp[0])
            kb.wait("q", dsp[1])
            for k in range(16):
                kb.dma("q", lambda q, k=k, t0=t0, sl=sl: q.dma_start(
                    out=yrT[sl][:, k * BLK:(k + 1) * BLK],
                    in_=d_yr[t0:t0 + BLK, k * 128:(k + 1) * 128], transpose=True), inc=d_tr)
            kb.wait("t", d_tr)
            for d in range(8):
                ps, evs = (pA, evA) if d % 2 == 0 else (pB, evB)
                kb.wait("t", evs)
                for k in range(16):
                    kb.op("t", lambda t, ps=ps, d=d, k=k, sl=sl, f=(k == 0), l=(k == 15):
                          t.matmul(ps[:, :],
                                   woq[:, k * DM + d * 128:k * DM + (d + 1) * 128],
                                   yrT[sl][:, k * BLK:(k + 1) * BLK], start=f, stop=l),
                          inc=(s_t if k == 15 else None))
                kb.wait("v", s_t)
                kb.op("v", lambda v, ps=ps, d=d, t0=t0: v.tensor_add(
                    sT[:, d * T + t0:d * T + t0 + BLK], ps[:, :],
                    xnT2[:, d * T + t0:d * T + t0 + BLK]), inc=evs)
        if "sres" in TAPS:
            kb.wait("g", evA)
            kb.wait("g", evB)
            for d in range(8):
                kb.dma("g", lambda g, d=d: g.dma_start(
                    out=d_sdbg[d * 128:(d + 1) * 128, :], in_=sT[:, d * T:(d + 1) * T]),
                    inc=d_sp0)
        ph5.close()

        kb.barrier(s_bar, dma_sems=(d_sp0, d_sp1, d_tr, d_w, d_cc))

        ph6 = ExitStack()
        w2q = ph6.enter_context(nc.sbuf_tensor("w2q", [128, 32 * DM], BF16))
        w2st = [ph6.enter_context(nc.sbuf_tensor(f"w2st{i}", [128, DM], F32)) for i in range(2)]
        hqb = ph6.enter_context(nc.sbuf_tensor("hqb", [128, 32 * BLK], BF16))
        hstg = [ph6.enter_context(nc.sbuf_tensor(f"hstg{i}", [128, BLK], F32)) for i in range(2)]
        qstg = ph6.enter_context(nc.sbuf_tensor("qstg", [128, BLK], F32))
        gstg = ph6.enter_context(nc.sbuf_tensor("gstg", [128, BLK], F32))
        fstg = [ph6.enter_context(nc.sbuf_tensor(f"fstg{i}", [128, BLK], BF16)) for i in range(2)]
        gC8 = ph6.enter_context(nc.sbuf_tensor("gC8", [1, 8], F32))
        fT = [ph6.enter_context(nc.sbuf_tensor(f"fT{i}", [128, DM], BF16, is_transpose_dest=True)
                                ) for i in range(2)]
        xin6 = [ph6.enter_context(nc.sbuf_tensor(f"xin6{i}", [128, DM], F32)) for i in range(2)]


        kb.wait("g", c_C, 1)
        kb.dma("g", lambda g: g.dma_start(
            out=gC8[:, :], in_=bap(cC_out.ap().tensor, [[1, 1], [16 * 128, 8]]),
            allow_slow_non_contiguous=True), inc=d_cc)
        kb.wait("v", d_cc)
        kb.vsync(s_v, s_s)
        kb.op("v", lambda v: v.tensor_reduce(vals[0:1, 11:12], gC8[:, :], AX.X, ALU.max))
        kb.op("v", lambda v: v.tensor_reduce(stats[:, 9:10], wm32[:, :], AX.X, ALU.max),
              inc=s_v)
        kb.wait("t", s_v)
        kb.op("t", lambda t: t.transpose(pH[0:1, 0:128], stats[:, 9:10], idn_f[:, :]), inc=s_t)
        kb.vsync(s_v, s_s)
        kb.wait("v", s_t)
        kb.op("v", lambda v: v.tensor_reduce(vals[0:1, 10:11], pH[0:1, 0:128], AX.X, ALU.max))
        kb.op("v", lambda v: v.tensor_scalar_max(vals[0:1, 11:12], vals[0:1, 11:12], 1e-6),
              inc=s_v)
        kb.vsync(s_v, s_s)
        kb.op("v", lambda v: v.tensor_scalar_max(vals[0:1, 10:11], vals[0:1, 10:11], 1e-6))
        kb.op("v", lambda v: v.tensor_scalar_mul(vals[0:1, 12:13], vals[0:1, 11:12], R127),
              inc=s_v)
        kb.vsync(s_v, s_s)
        kb.op("v", lambda v: v.tensor_scalar_mul(vals[0:1, 14:15], vals[0:1, 10:11], R127))
        kb.op("v", lambda v: v.reciprocal(vals[0:1, 13:14], vals[0:1, 12:13]), inc=s_v)
        kb.vsync(s_v, s_s)
        kb.op("v", lambda v: v.reciprocal(vals[0:1, 15:16], vals[0:1, 14:15]))
        kb.op("v", lambda v: v.tensor_mul(vals[0:1, 9:10], vals[0:1, 12:13],
                                          vals[0:1, 14:15]), inc=s_v)
        kb.vsync(s_v, s_s)
        kb.wait("g", s_v)
        kb.dma("g", lambda g: g.dma_start(out=d_val.ap(), in_=vals[:, :]), inc=d_cc)
        kb.wait("g", d_cc)
        for slot, ci in [(13, 3), (15, 4), (9, 5)]:
            kb.dma("g", lambda g, slot=slot, ci=ci: g.dma_start(
                out=cols[:, ci:ci + 1],
                in_=bap(d_val.ap().tensor, [[0, 128], [1, 1]], offset=slot)), inc=d_cc)
        kb.wait("v", d_cc)
        kb.op("v", lambda v: v.tensor_scalar_mul(b2s[:, :], b2sb[:, :], cols[:, 5:6]), inc=s_v)
        for k in range(32):
            sl = k % 2
            kb.wait("g", s_v)
            kb.dma("g", lambda g, k=k, sl=sl: g.dma_start(
                out=w2st[sl][:, :], in_=w2_in[k * 128:(k + 1) * 128, :]), inc=d_w)
            kb.wait("v", d_w)
            kb.op("v", lambda v, sl=sl: v.tensor_scalar(
                w2st[1 - sl][:, :], w2st[sl][:, :], cols[:, 4:5], MAGIC, ALU.mult, ALU.add))
            kb.op("v", lambda v, k=k, sl=sl: v.tensor_scalar(
                w2q[:, k * DM:(k + 1) * DM], w2st[1 - sl][:, :], MAGIC, None, ALU.subtract),
                inc=s_v)
        kb.wait("g", dsp[0])
        kb.wait("g", dsp[1])
        for b in range(NBLK):
            t0 = b * BLK
            for k in range(32):
                sl = k % 2
                kb.wait("g", s_v)
                kb.dma("g", lambda g, k=k, t0=t0, sl=sl: g.dma_start(
                    out=hstg[sl][:, :], in_=d_h[k * 128:(k + 1) * 128, t0:t0 + BLK]),
                    inc=dsp[sl])
                kb.wait("s", dsp[sl])
                kb.op("s", lambda s, sl=sl: s.activation(
                    qstg[:, :], hstg[sl][:, :], AF.Copy, bias=MAGIC, scale=cols[:, 3:4]),
                    inc=s_s)
                kb.wait("v", s_s)
                kb.op("v", lambda v, k=k: v.tensor_scalar(
                    hqb[:, k * BLK:(k + 1) * BLK], qstg[:, :], MAGIC, None, ALU.subtract),
                    inc=s_v)
            kb.wait("t", s_v)
            for d in range(8):
                ps, evs = (pA, evA) if d % 2 == 0 else (pB, evB)
                kb.wait("t", evs)
                for k in range(32):
                    kb.op("t", lambda t, ps=ps, d=d, k=k, f=(k == 0), l=(k == 31):
                          t.matmul(ps[:, :],
                                   w2q[:, k * DM + d * 128:k * DM + (d + 1) * 128],
                                   hqb[:, k * BLK:(k + 1) * BLK], start=f, stop=l),
                          inc=(s_t if k == 31 else None))
                sl = d % 2
                kb.wait("s", s_t)
                kb.op("s", lambda s, ps=ps, d=d: s.activation(
                    gstg[:, :], ps[:, :], AF.Silu, bias=b2s[:, d:d + 1], scale=cols[:, 5:6]),
                    inc=s_s)
                kb.wait("v", s_s)
                kb.wait("v", dsp[sl])
                kb.op("v", lambda v, d=d, t0=t0, sl=sl: v.tensor_mul(
                    fstg[sl][:, :], gstg[:, :], sT[:, d * T + t0:d * T + t0 + BLK]), inc=evs)
                kb.wait("g", evs)
                kb.dma("g", lambda g, d=d, t0=t0, sl=sl: g.dma_start(
                    out=d_F[d * 128:(d + 1) * 128, t0:t0 + BLK], in_=fstg[sl][:, :]),
                    inc=dsp[sl])
        fin_ld = [d_ld0, d_ld1]
        fin_st = [S("fin_st0"), S("fin_st1")]
        kb.wait("q", dsp[0])
        kb.wait("q", dsp[1])
        for ci in range(NCHK):
            c0 = ci * CHK
            sl = ci % 2
            kb.wait("q", fin_st[sl])
            kb.dma("q", lambda q, c0=c0, sl=sl: q.dma_start(
                out=fT[sl][:, :], in_=d_F[:, c0:c0 + CHK], transpose=True), inc=fin_ld[sl])
            kb.dma("q", lambda q, c0=c0, sl=sl: q.dma_start(
                out=xin6[sl][:, :], in_=x_in[c0:c0 + CHK, :]), inc=fin_ld[sl])
            kb.wait("v", fin_ld[sl])
            kb.wait("v", fin_st[sl])
            kb.op("v", lambda v, sl=sl: v.tensor_add(xin6[sl][:, :], xin6[sl][:, :],
                                                     fT[sl][:, :]), inc=s_v)
            kb.wait("g", s_v)
            kb.dma("g", lambda g, c0=c0, sl=sl: g.dma_start(
                out=out_t[c0:c0 + CHK, :], in_=xin6[sl][:, :]), inc=fin_st[sl])
            fin_st[sl].n
            s_out.n
        kb.wait("q", fin_st[0])
        kb.wait("q", fin_st[1])
        for name, tup in taps.items():
            kb.wait("g", fin_st[0])
            kb.wait("g", fin_st[1])
            if len(tup) == 3:
                tout, src, row = tup
                kb.dma("g", lambda g, tout=tout, src=src, row=row: g.dma_start(
                    out=tout.ap(), in_=src[row:row + 128, :]), inc=s_tap)
            else:
                tout, src = tup
                kb.dma("g", lambda g, tout=tout, src=src: g.dma_start(
                    out=tout.ap(), in_=src.ap()), inc=s_tap)
        kb.wait("g", s_tap)
        ph6.close()

        with nc.Block() as block:
            block.tensor(lambda e: [f(e) for f in kb.streams["t"]])
            block.vector(lambda e: [f(e) for f in kb.streams["v"]])
            block.scalar(lambda e: [f(e) for f in kb.streams["s"]])
            block.gpsimd(lambda e: [f(e) for f in kb.streams["g"]])
            block.sync(lambda e: [f(e) for f in kb.streams["q"]])

    return nc


_NC_CACHE = {}


def _get_nc():
    key = tuple(TAPS)
    if key not in _NC_CACHE:
        _NC_CACHE[key] = build_nc()
    return _NC_CACHE[key]


def make_inmaps(inputs):
    x = np.asarray(inputs["L0_attn_x"], np.float32)
    f32 = lambda a: np.ascontiguousarray(np.asarray(a, np.float32))
    w_inT = f32(np.pad(np.asarray(inputs["L0_attn_in_proj_w"], np.float32).T,
                       ((0, 0), (0, DIP - 4384))))
    woT = f32(np.asarray(inputs["L0_attn_out_proj_w"], np.float32).T)
    w1T = f32(np.asarray(inputs["L0_ffn_w1"], np.float32).T)
    w2T = f32(np.asarray(inputs["L0_ffn_w2"], np.float32).T)
    convw = f32(np.asarray(inputs["L0_attn_conv_w"], np.float32)
                .reshape(18, 128, 4).transpose(1, 0, 2).reshape(128, 72))
    convb = f32(np.asarray(inputs["L0_attn_conv_b"], np.float32).reshape(18, 128).T)
    lnw = f32(np.asarray(inputs["L0_attn_ln_w"], np.float32).reshape(KD, 128).T)
    lnb = f32(np.asarray(inputs["L0_attn_ln_b"], np.float32).reshape(KD, 128).T)
    dtb = f32(np.asarray(inputs["L0_attn_dt_bias"], np.float32).reshape(32, 1))
    alog = f32(np.asarray(inputs["L0_attn_A_log"], np.float32).reshape(32, 1))
    dvec = f32(np.asarray(inputs["L0_attn_D"], np.float32).reshape(1, 32))
    nw = f32(np.asarray(inputs["L0_attn_norm_w"], np.float32).reshape(16, 128).T)
    b1c = f32(np.asarray(inputs["L0_ffn_b1"], np.float32).reshape(32, 128).T)
    b2c = f32(np.asarray(inputs["L0_ffn_b2"], np.float32).reshape(8, 128).T)
    in_maps = []
    for k in range(8):
        b, s = k // 4, k % 4
        xs = f32(x[b, s * T:(s + 1) * T])
        if s == 0:
            xh = np.zeros((3, DM), np.float32)
            hm = np.zeros((1, 1), np.float32)
        else:
            xh = f32(x[b, s * T - 3:s * T])
            hm = np.ones((1, 1), np.float32)
        mm = np.zeros((8, 8), np.float32)
        mk = np.zeros((8, 1), np.float32)
        for r in range(s):
            mk[r, 0] = 1.0
            for i in range(r + 1, s):
                mm[4 * b + i, r] = 1.0
        in_maps.append({
            "x": xs, "xh": xh, "hm": hm, "mmat": mm, "mmask": mk,
            "w_in": w_inT, "wo": woT, "w1": w1T, "w2": w2T,
            "convw": convw, "convb": convb, "lnw": lnw, "lnb": lnb,
            "dtb": dtb, "alog": alog, "dvec": dvec, "nw": nw,
            "b1c": b1c, "b2c": b2c,
        })
    return in_maps


def kernel(**inputs):
    nc = _get_nc()
    in_maps = make_inmaps(inputs)
    res = run_bass_kernel_spmd(nc, in_maps, core_ids=list(range(8)))
    x = np.asarray(inputs["L0_attn_x"], np.float32)
    out = np.empty_like(x)
    for k in range(8):
        b, s = k // 4, k % 4
        out[b, s * T:(s + 1) * T] = np.asarray(res.results[k]["out"], np.float32)
    return out


# version 13 (final)
# speedup vs baseline: 1.3732x; 1.3732x over previous
import numpy as np
import ml_dtypes
from contextlib import ExitStack

import concourse.bass as bass
import concourse.mybir as mybir
from concourse.bass import AP
from concourse.bass_utils import run_bass_kernel_spmd

F32 = mybir.dt.float32
BF16 = mybir.dt.bfloat16
AF = mybir.ActivationFunctionType
ALU = mybir.AluOpType
AX = mybir.AxisListType

MAGIC = float(np.float32(1.5 * 2.0 ** 23))
NEG = -1.0e30
EPS = 1e-5
R127 = float(np.float32(1.0 / 127.0))

DM, DI, NH, HD, DS = 1024, 2048, 32, 64, 128
CONVD, DIP, FFN = 2304, 4480, 4096
T, NBLK, BLK, NCHK, CHK = 2048, 4, 512, 16, 128
KD = DM // 128
ENG = ("t", "v", "s", "g", "q")

TAPS = []


class Sem:
    __slots__ = ("h", "n", "name")

    def __init__(self, h, name):
        self.h, self.n, self.name = h, 0, name


class KB:
    def __init__(self, nc, stack):
        self.nc, self.stack = nc, stack
        self.streams = {e: [] for e in ENG}
        self.nins = 0

    def sem(self, name):
        return Sem(self.stack.enter_context(self.nc.semaphore(name)), name)

    def emit(self, eng, fn):
        self.streams[eng].append(fn)
        self.nins += 1

    def op(self, eng, fn, inc=None, n=1):
        if inc is not None:
            inc.n += n
            self.emit(eng, lambda e, fn=fn, h=inc.h, n=n: fn(e).then_inc(h, n))
        else:
            self.emit(eng, fn)

    def dma(self, eng, fn, inc):
        self.op(eng, fn, inc=inc, n=16)

    def wait(self, eng, sem, val=None):
        v = sem.n if val is None else val
        if v > 0:
            self.emit(eng, lambda e, h=sem.h, v=v: e.wait_ge(h, v))

    def vsync(self, sv, ss):
        self.wait("s", sv)
        self.op("s", lambda s: s.nop(), inc=ss)
        self.wait("v", ss)

    def barrier(self, sem, engines=ENG, dma_sems=()):
        base = sem.n
        for e in engines:
            self.op(e, lambda en: en.nop(), inc=sem)
        for e in engines:
            self.wait(e, sem, base + len(engines))
            for ds in dma_sems:
                self.wait(e, ds)


def bap(t, dims, offset=0):
    return AP(t, offset, [list(p) for p in dims])


def build_nc():
    nc = bass.Bass("TRN2", target_bir_lowering=False)
    dt = {}

    def din(name, shape, dtype=F32):
        dt[name] = nc.dram_tensor(name, shape, dtype, kind="ExternalInput")
        return dt[name]

    def dscr(name, shape, dtype):
        dt[name] = nc.dram_tensor(name, shape, dtype)
        return dt[name]

    x_in = din("x", [T, DM])
    din("xh", [3, DM])
    din("hm", [1, 1])
    din("mmat", [8, 8])
    din("mmask", [8, 1])
    w_in = din("w_in", [DM, DIP])
    wo_in = din("wo", [DI, DM])
    w1_in = din("w1", [DM, FFN])
    w2_in = din("w2", [FFN, DM])
    din("convw", [128, 72])
    din("convb", [128, 18])
    din("lnw", [128, KD])
    din("lnb", [128, KD])
    din("dtb", [32, 1])
    din("alog", [32, 1])
    din("dvec", [1, 32])
    din("nw", [128, 16])
    din("b1c", [128, 32])
    din("b2c", [128, 8])
    out_t = nc.dram_tensor("out", [T, DM], F32, kind="ExternalOutput")

    d_xn = dscr("d_xn", [T, DM], BF16)
    d_x2 = dscr("d_x2", [T, DM], BF16)
    d_z = dscr("d_z", [DI, T], BF16)
    d_xc = dscr("d_xc", [DI, T], BF16)
    d_sr = dscr("d_sr", [32, T], F32)
    d_srf = dscr("d_srf", [1, NCHK * 4096], F32)
    d_h = dscr("d_h", [FFN, T], F32)
    d_hq = dscr("d_hq", [FFN, T], BF16)
    d_xnf = dscr("d_xnf", [T, DM], F32)
    d_xq = dscr("d_xq", [T, DM], BF16)
    d_yr = dscr("d_yr", [T, DI], BF16)
    d_F = dscr("d_F", [DM, T], BF16)
    d_sdbg = dscr("d_sdbg", [DM, T], BF16)
    d_val = dscr("d_val", [1, 16], F32)
    d_coef = dscr("d_coef", [8, 32], F32)
    cA_in = dscr("cA_in", [16, 128], F32)
    cA_out = dscr("cA_out", [128, 128], F32)
    cB_in = dscr("cB_in", [16, 128], F32)
    cB_out = dscr("cB_out", [128, 128], F32)
    cC_in = dscr("cC_in", [16, 128], F32)
    cC_out = dscr("cC_out", [128, 128], F32)
    cD_in = dscr("cD_in", [128, DI], BF16)
    cD_out = dscr("cD_out", [512, DI], BF16)

    tap_windows = {"xnt0": (d_xn, 0), "xnt1": (d_xn, 128), "xnt15": (d_xn, 1920)}
    tap_shapes = {"xn": ([T, DM], d_xn, BF16), "x2": ([T, DM], d_x2, BF16),
                  "z": ([DI, T], d_z, BF16), "xc": ([DI, T], d_xc, BF16),
                  "srel": ([32, T], d_sr, F32), "h": ([FFN, T], d_h, F32),
                  "hq": ([FFN, T], d_hq, BF16),
                  "yr": ([T, DI], d_yr, BF16), "F": ([DM, T], d_F, BF16),
                  "sres": ([DM, T], d_sdbg, BF16),
                  "hseg": ([128, DI], cD_in, BF16), "val": ([1, 16], d_val, F32)}
    taps = {}
    for name in TAPS:
        if name in tap_windows:
            tsrc, row = tap_windows[name]
            taps[name] = (nc.dram_tensor("tap_" + name, [128, DM], BF16,
                                         kind="ExternalOutput"),
                          tsrc, row)
            continue
        shp, src, tdt = tap_shapes[name]
        taps[name] = (nc.dram_tensor("tap_" + name, shp, tdt, kind="ExternalOutput"), src)

    stack = ExitStack()
    with stack:
        kb = KB(nc, stack)
        sb = lambda *a, **k: stack.enter_context(nc.sbuf_tensor(*a, **k))
        psu = lambda *a, **k: stack.enter_context(nc.psum_tensor(*a, **k))

        pA = psu("pA", [128, 512], F32)
        pB = psu("pB", [128, 512], F32)
        pW = psu("pW", [128, 2048], F32)
        pH = psu("pH", [128, 384], F32)
        pS = psu("pS", [128, 512], F32)

        S = lambda n: kb.sem(n)
        s_t, s_v, s_s, s_g, s_bar = S("s_t"), S("s_v"), S("s_s"), S("s_g"), S("s_bar")
        d_small, d_win, d_x, d_tr, d_w = S("d_small"), S("d_win"), S("d_x"), S("d_tr"), S("d_w")
        d_sp0, d_sp1 = S("d_sp0"), S("d_sp1")
        d_ld0, d_ld1 = S("d_ld0"), S("d_ld1")
        d_cc = S("d_cc")
        d_x2t = [S("d_x2t0"), S("d_x2t1")]
        d_xtk = [S("d_xtk0"), S("d_xtk1")]
        c_A, c_B, c_C, c_D = S("c_A"), S("c_B"), S("c_C"), S("c_D")
        s_out, s_tap = S("s_out"), S("s_tap")
        evA, evB, evW, evS = S("evA"), S("evB"), S("evW"), S("evS")

        idn_b = sb("idn_b", [128, 128], BF16)
        idn_f = sb("idn_f", [128, 128], F32)
        onesb = sb("onesb", [1, 128], BF16)
        BT = sb("BT", [DS, T], BF16)
        CTt = sb("CTt", [DS, T], BF16)
        srel = sb("srel", [32, T], F32)
        ldms = sb("ldms", [32, T], F32)
        elog = sb("elog", [32, 1], F32)
        dtsc = sb("dtsc", [32, 4], F32)
        hst = sb("hst", [DS, DI], BF16)
        DT32 = sb("DT32", [128, 32], BF16)
        cwgt = sb("cwgt", [128, 72], F32)
        cbia = sb("cbia", [128, 18], F32)
        lnwc = sb("lnwc", [128, KD], F32)
        lnbc = sb("lnbc", [128, KD], F32)
        lnbb = sb("lnbb", [128, KD], BF16)
        dtbc = sb("dtbc", [32, 1], F32)
        alogc = sb("alogc", [32, 1], F32)
        nac = sb("nac", [32, 1], F32)
        nwc = sb("nwc", [128, 16], F32)
        b1sb = sb("b1sb", [128, 32], F32)
        b2sb = sb("b2sb", [128, 8], F32)
        biasj = sb("biasj", [128, 35], F32)
        b34 = sb("b34", [32, 1], F32)
        mmatb = sb("mmatb", [8, 8], BF16)
        mmaskc = sb("mmaskc", [8, 1], F32)
        mres = sb("mres", [32, BLK], F32)
        vals = sb("vals", [1, 16], F32)
        cols = sb("cols", [128, 8], F32)
        maxxn = sb("maxxn", [128, 1], F32)
        mx16 = sb("mx16", [128, 16], F32)
        wm32 = sb("wm32", [128, 32], F32)
        wmB = sb("wmB", [128, 32], F32)

        maxh = sb("maxh", [128, 1], F32)
        b1s = sb("b1s", [128, 32], F32)
        b2s = sb("b2s", [128, 8], F32)
        stats = sb("stats", [128, 12], F32)
        kconst = sb("kconst", [128, 3], F32)
        nc.const_aps.register(F32, EPS, kconst[:, 0:1].opt())
        nc.const_aps.register(F32, 1.0, kconst[:, 1:2].opt())
        nc.const_aps.register(F32, MAGIC, kconst[:, 2:3].opt())
        hstat = sb("hstat", [3, 12], F32)
        payl = sb("payl", [16, 128], F32)
        gC8 = sb("gC8", [1, 8], F32)

        def ln_tile(src_ap_fn, p, fd, out_fn, out2_fn=None, absmax_col=None,
                    out32_fn=None):
            st = stats if p == 128 else hstat
            kb.op("v", lambda v: v.tensor_reduce(st[0:p, 0:1], src_ap_fn(), AX.X,
                                                 ALU.add), inc=s_v)
            kb.wait("s", s_v)
            kb.op("s", lambda s: s.activation(sq_scr[0:p, 0:fd], src_ap_fn(), AF.Square,
                                              accum_out=st[0:p, 1:2]))
            kb.op("s", lambda s: s.activation(st[0:p, 3:4], st[0:p, 0:1], AF.Square,
                                              scale=1.0 / fd), inc=s_s)
            kb.wait("v", s_s)
            kb.op("v", lambda v: v.scalar_tensor_tensor(
                st[0:p, 4:5], st[0:p, 1:2], 1.0 / fd, st[0:p, 3:4],
                ALU.mult, ALU.subtract), inc=s_v)
            kb.wait("s", s_v)
            kb.op("s", lambda s: s.activation(st[0:p, 5:6], st[0:p, 4:5], AF.Ln, bias=EPS))
            kb.op("s", lambda s: s.activation(st[0:p, 2:3], st[0:p, 0:1], AF.Copy,
                                              scale=1.0 / fd))
            kb.op("s", lambda s: s.activation(st[0:p, 6:7], st[0:p, 5:6], AF.Exp,
                                              scale=-0.5), inc=s_s)
            kb.wait("v", s_s)
            if out32_fn is None:
                kb.op("v", lambda v: v.tensor_scalar(out_fn(), src_ap_fn(), st[0:p, 2:3],
                                                     st[0:p, 6:7], ALU.subtract, ALU.mult),
                      inc=s_v)
            else:
                kb.op("v", lambda v: v.tensor_scalar(out32_fn(), src_ap_fn(), st[0:p, 2:3],
                                                     st[0:p, 6:7], ALU.subtract, ALU.mult))
                kb.op("v", lambda v: v.tensor_copy(out_fn(), out32_fn()), inc=s_v)
                if absmax_col is not None:
                    kb.op("v", lambda v: v.reduce_max(absmax_col, out32_fn(), AX.X,
                                                      apply_absolute_value=True), inc=s_v)
            if out2_fn is not None:
                ln_tile(out32_fn if out32_fn is not None else out_fn, p, fd, out2_fn)

        def part_max_to_val(col, slot):
            kb.wait("t", s_v)
            kb.op("t", lambda t: t.transpose(pH[0:1, 0:128], col[:, :], idn_f[:, :]), inc=s_t)
            kb.wait("v", s_t)
            kb.op("v", lambda v: v.tensor_reduce(vals[0:1, slot:slot + 1], pH[0:1, 0:128],
                                                 AX.X, ALU.max), inc=s_v)

        sq_scr = sb("sq_scr", [128, 2048], F32)
        phT = ExitStack()
        tcolA = phT.enter_context(nc.sbuf_tensor("tcolA", [128, NCHK * 32], F32))
        estA = phT.enter_context(nc.sbuf_tensor("estA", [128, NCHK * 32], F32))
        ebcA = phT.enter_context(nc.sbuf_tensor("ebcA", [128, NCHK * 32], BF16))
        e2tA = phT.enter_context(nc.sbuf_tensor("e2tA", [128, NCHK * 32], BF16))
        btokA = phT.enter_context(nc.sbuf_tensor("btokA", [128, NCHK * 128], BF16))
        phW = ExitStack()
        W = phW.enter_context(nc.sbuf_tensor("W", [128, KD * DIP], BF16))
        hal = phW.enter_context(nc.sbuf_tensor("hal", [128, 18 * 3], BF16))
        xhal = phW.enter_context(nc.sbuf_tensor("xhal", [3, DM], F32))
        xh1 = phW.enter_context(nc.sbuf_tensor("xh1", [3, DM], F32))
        xh2 = phW.enter_context(nc.sbuf_tensor("xh2", [3, DM], F32))
        uhT = phW.enter_context(nc.sbuf_tensor("uhT", [128, KD * 3], BF16))

        for nm, tgt in [("convw", cwgt), ("convb", cbia), ("lnw", lnwc),
                        ("lnb", lnbc), ("dtb", dtbc), ("alog", alogc), ("nw", nwc),
                        ("b1c", b1sb), ("b2c", b2sb), ("mmask", mmaskc)]:
            kb.dma("g", lambda g, nm=nm, tgt=tgt: g.dma_start(out=tgt[:, :], in_=dt[nm].ap()),
                   inc=d_small)
        kb.dma("g", lambda g: g.dma_start(out=mmatb[:, :], in_=dt["mmat"].ap()), inc=d_small)
        kb.dma("g", lambda g: g.dma_start(
            out=DT32[:, :], in_=bap(dt["dvec"].ap().tensor, [[0, 128], [1, 32]])), inc=d_small)
        kb.dma("g", lambda g: g.dma_start(
            out=cols[:, 6:7], in_=bap(dt["hm"].ap().tensor, [[0, 128], [1, 1]])), inc=d_small)
        kb.dma("g", lambda g: g.dma_start(out=xhal[:, :], in_=dt["xh"].ap()), inc=d_small)
        for k in range(KD):
            kb.dma("g", lambda g, k=k: g.dma_start(
                out=W[:, k * DIP:(k + 1) * DIP], in_=w_in[k * 128:(k + 1) * 128, :]), inc=d_win)

        kb.wait("g", d_small)
        kb.op("g", lambda g: g.memset(kconst[:, 0:1], EPS))
        kb.op("g", lambda g: g.memset(kconst[:, 1:2], 1.0))
        kb.op("g", lambda g: g.memset(kconst[:, 2:3], MAGIC))
        kb.op("g", lambda g: g.memset(onesb[:, :], 1.0))
        kb.op("g", lambda g: g.memset(idn_f[:, :], 1.0))
        kb.op("g", lambda g: g.affine_select(idn_f[:, :], idn_f[:, :], [[-1, 128]],
                                             ALU.is_equal, 0.0, base=0, channel_multiplier=1))
        kb.op("g", lambda g: g.memset(mres[:, :], 1.0))
        kb.op("g", lambda g: g.memset(
            bap(mres.ap().tensor, [[mres.ap().ap[0][0], 32], [CHK, NBLK], [1, 1]],
                offset=mres.ap().offset), 0.0))
        kb.op("g", lambda g: g.memset(elog[:, :], 0.0))
        kb.op("g", lambda g: g.memset(maxxn[:, :], 0.0))
        kb.op("g", lambda g: g.memset(maxh[:, :], 0.0))
        kb.op("g", lambda g: g.memset(hst[:, :], 0.0))
        kb.op("g", lambda g: g.memset(payl[:, :], 0.0))
        kb.op("g", lambda g: g.memset(vals[:, :], 0.0), inc=s_g)
        g_const = s_g.n
        kb.wait("v", s_g, g_const)
        kb.op("v", lambda v: v.tensor_copy(idn_b[:, :], idn_f[:, :]))
        kb.op("v", lambda v: v.tensor_copy(lnbb[:, :], lnbc[:, :]), inc=s_v)
        v_prep = s_v.n
        kb.wait("s", s_g, g_const)
        kb.op("s", lambda s: s.activation(nac[:, :], alogc[:, :], AF.Exp), inc=s_s)
        kb.wait("v", s_s)
        kb.op("v", lambda v: v.tensor_scalar_mul(nac[:, :], nac[:, :], -1.0), inc=s_v)

        kb.wait("t", d_win)
        kb.wait("t", s_v, v_prep)
        for k in range(KD):
            for j in range(35):
                first = (k == 0 and j == 0)
                last = (k == KD - 1 and j == 34)
                kb.op("t", lambda t, k=k, j=j, f=first, l=last: t.matmul(
                    pA[:, j:j + 1], W[:, k * DIP + j * 128:k * DIP + (j + 1) * 128],
                    lnbb[:, k:k + 1], start=f, stop=l), inc=(s_t if last else None))
        kb.wait("v", s_t)
        kb.op("v", lambda v: v.tensor_add(b34[:, :], pA[0:32, 34:35], dtbc[:, :]))
        kb.op("v", lambda v: v.tensor_copy(biasj[:, :], pA[:, 0:35]), inc=evA)
        kb.op("v", lambda v: v.nop(), inc=s_v)
        for k in range(KD):
            kb.op("v", lambda v, k=k: v.tensor_scalar_mul(
                W[:, k * DIP:(k + 1) * DIP], W[:, k * DIP:(k + 1) * DIP], lnwc[:, k:k + 1]),
                inc=s_v)
        v_wsc = s_v.n

        kb.wait("v", d_small)
        ln_tile(lambda: xhal[:, :], 3, DM, lambda: xh1[:, :],
                out2_fn=lambda: xh2[:, :])
        v_halo_ln = s_v.n
        kb.wait("t", s_v, v_halo_ln)
        kb.wait("t", evA)
        for k in range(KD):
            kb.op("t", lambda t, k=k: t.transpose(
                pH[:, 352:355],
                xh2[0:3, k * 128:(k + 1) * 128], idn_f[0:3, 0:3]), inc=s_t)
            kb.wait("v", s_t)
            kb.op("v", lambda v, k=k: v.tensor_copy(uhT[:, k * 3:(k + 1) * 3],
                                                    pH[:, 352:355]), inc=s_v)
            kb.wait("t", s_v)
        kb.wait("t", s_v, v_wsc)
        for ji, j in enumerate(range(16, 34)):
            for k in range(KD):
                kb.op("t", lambda t, j=j, k=k, f=(k == 0), l=(k == KD - 1): t.matmul(
                    pA[:, 40:43],
                    W[:, k * DIP + j * 128:k * DIP + (j + 1) * 128],
                    uhT[:, k * 3:(k + 1) * 3], start=f, stop=l),
                    inc=(s_t if k == KD - 1 else None))
            kb.wait("v", s_t)
            kb.op("v", lambda v, ji=ji, j=j: v.tensor_scalar(
                hal[:, ji * 3:(ji + 1) * 3], pA[:, 40:43], biasj[:, j:j + 1],
                cols[:, 6:7], ALU.add, ALU.mult), inc=s_v)
            kb.wait("t", s_v)
        v_halo = s_v.n

        ph1 = ExitStack()
        xt = [ph1.enter_context(nc.sbuf_tensor(f"xt{i}", [128, DM], F32)) for i in range(2)]
        xnb = [ph1.enter_context(nc.sbuf_tensor(f"xnb{i}", [128, DM], BF16)) for i in range(2)]
        xn32 = [ph1.enter_context(nc.sbuf_tensor(f"xn32{i}", [128, DM], F32)) for i in range(2)]
        x2b = [ph1.enter_context(nc.sbuf_tensor(f"x2b{i}", [128, DM], BF16)) for i in range(2)]
        dxs = [d_ld0, d_ld1]
        dsp = [d_sp0, d_sp1]
        for i in range(16):
            sl = i % 2
            kb.wait("q", dsp[sl])
            kb.dma("q", lambda q, i=i, sl=sl: q.dma_start(
                out=xt[sl][:, :], in_=x_in[i * 128:(i + 1) * 128, :]), inc=dxs[sl])
            kb.wait("v", dxs[sl])
            kb.wait("v", dsp[sl])
            kb.wait("s", dxs[sl])
            ln_tile(lambda sl=sl: xt[sl][:, :], 128, DM, lambda sl=sl: xnb[sl][:, :],
                    out2_fn=lambda sl=sl: x2b[sl][:, :], absmax_col=mx16[:, i:i + 1],
                    out32_fn=lambda sl=sl: xn32[sl][:, :])
            kb.wait("g", s_v)
            kb.dma("g", lambda g, i=i, sl=sl: g.dma_start(
                out=d_xn[i * 128:(i + 1) * 128, :], in_=xnb[sl][:, :]), inc=dsp[sl])
            kb.dma("g", lambda g, i=i, sl=sl: g.dma_start(
                out=d_xnf[i * 128:(i + 1) * 128, :], in_=xn32[sl][:, :]), inc=dsp[sl])
            kb.dma("g", lambda g, i=i, sl=sl: g.dma_start(
                out=d_x2[i * 128:(i + 1) * 128, :], in_=x2b[sl][:, :]), inc=dsp[sl])
        kb.vsync(s_v, s_s)
        kb.op("v", lambda v: v.tensor_reduce(maxxn[:, :], mx16[:, :], AX.X, ALU.max),
              inc=s_v)
        part_max_to_val(maxxn, 0)
        kb.wait("s", s_v)
        kb.op("s", lambda s: s.copy(payl[0:1, 0:1], vals[0:1, 0:1]), inc=s_s)
        kb.wait("g", s_s)
        kb.dma("g", lambda g: g.dma_start(out=cA_in.ap(), in_=payl[:, :]), inc=d_cc)
        kb.wait("g", d_cc)
        kb.op("g", lambda g: g.collective_compute(
            "AllGather", ALU.bypass, replica_groups=[list(range(8))],
            ins=[cA_in.ap().opt()], outs=[cA_out.ap().opt()]), inc=c_A, n=1)

        ph1.close()

        ph2 = ExitStack()
        x2T = [ph2.enter_context(nc.sbuf_tensor(f"x2T{i}", [128, KD * BLK], BF16))
               for i in range(2)]
        xbcb = ph2.enter_context(nc.sbuf_tensor("xbcb", [128, 18 * 515], BF16))
        zst = [ph2.enter_context(nc.sbuf_tensor(f"zst{i}", [128, BLK], BF16)) for i in range(2)]
        cvt = [ph2.enter_context(nc.sbuf_tensor(f"cvt{i}", [128, BLK], BF16)) for i in range(2)]
        cvo = [ph2.enter_context(nc.sbuf_tensor(f"cvo{i}", [128, BLK], BF16)) for i in range(2)]
        dtr = ph2.enter_context(nc.sbuf_tensor("dtr", [32, BLK], F32))
        tda = ph2.enter_context(nc.sbuf_tensor("tda", [32, BLK], F32))
        tdb = ph2.enter_context(nc.sbuf_tensor("tdb", [32, BLK], F32))
        dtv = ph2.enter_context(nc.sbuf_tensor("dtv", [32, BLK], F32))
        avb = ph2.enter_context(nc.sbuf_tensor("avb", [32, BLK], F32))
        xtok = [ph2.enter_context(nc.sbuf_tensor(f"xtok{i}", [128, DI], BF16)) for i in range(2)]
        xsc = ph2.enter_context(nc.sbuf_tensor("xsc", [128, DI], BF16))
        htmp = ph2.enter_context(nc.sbuf_tensor("htmp", [128, DI], BF16))
        esr = ph2.enter_context(nc.sbuf_tensor("esr", [1, 32], BF16))
        e2h = ph2.enter_context(nc.sbuf_tensor("e2h", [32, 128], F32))
        e2hb = ph2.enter_context(nc.sbuf_tensor("e2hb", [32, 128], F32))
        btf = ph2.enter_context(nc.sbuf_tensor("btf", [128, 128], F32))

        def ap3(tile, off, d1, d2):
            a = tile.ap()
            return bap(a.tensor, [a.ap[0], d1, d2], offset=a.offset + off)

        def emit_march(ci):
            c0 = ci * CHK
            tb = ci * 32
            kb.wait("t", s_v)
            kb.wait("t", s_s)
            kb.op("t", lambda t: t.transpose(pH[:, 256:288], ldms[:, c0:c0 + CHK],
                                             idn_f[0:32, 0:32]))
            kb.op("t", lambda t: t.transpose(pH[:, 288:320], srel[:, c0:c0 + CHK],
                                             idn_f[0:32, 0:32]), inc=s_t)
            kb.wait("s", s_t)
            kb.op("s", lambda s: s.copy(tcolA[:, tb:tb + 32], pH[:, 256:288]))
            kb.op("s", lambda s: s.activation(estA[:, tb:tb + 32], pH[:, 288:320], AF.Exp),
                  inc=s_s)
            kb.op("t", lambda t: t.transpose(pH[0:1, 128:160], srel[:, c0 + 127:c0 + 128],
                                             idn_f[0:32, 0:32]), inc=s_t)
            kb.wait("s", s_t)
            kb.op("s", lambda s: s.activation(esr[:, :], pH[0:1, 128:160], AF.Exp), inc=s_s)
            kb.wait("t", s_s)
            kb.op("t", lambda t: t.matmul(pH[:, 160:192], onesb[:, :], esr[:, :],
                                          start=True, stop=True), inc=s_t)
            kb.wait("s", s_t)
            kb.op("s", lambda s: s.copy(ebcA[:, tb:tb + 32], pH[:, 160:192]), inc=s_s)
            kb.op("v", lambda v: v.tensor_scalar_add(e2h[:, :], ldms[:, c0:c0 + CHK],
                                                     srel[:, c0 + 127:c0 + 128]), inc=s_v)
            kb.wait("s", s_v)
            kb.op("s", lambda s: s.activation(e2hb[:, :], e2h[:, :], AF.Exp))
            kb.op("s", lambda s, c0=c0: s.copy(btf[:, :], BT[:, c0:c0 + CHK]), inc=s_s)
            kb.wait("t", s_s)
            kb.op("t", lambda t: t.transpose(pH[:, 224:256], e2hb[:, :], idn_f[0:32, 0:32]),
                  inc=s_t)
            kb.op("t", lambda t: t.transpose(pH[:, 0:128], btf[:, :], idn_f[:, :]), inc=s_t)
            kb.wait("s", s_t)
            kb.op("s", lambda s: s.copy(e2tA[:, tb:tb + 32], pH[:, 224:256]))
            kb.op("s", lambda s: s.copy(btokA[:, ci * 128:(ci + 1) * 128], pH[:, 0:128]),
                  inc=s_s)
            sl = ci % 2
            kb.wait("g", s_s)
            kb.op("g", lambda g, sl=sl: g.tensor_tensor(
                xsc[:, :], xtok[sl][:, :], ap3(e2tA, tb, [1, 32], [0, 64]), ALU.mult),
                inc=s_g)
            kb.wait("t", s_g)
            kb.wait("t", evW)
            for q in range(4):
                kb.op("t", lambda t, ci=ci, q=q: t.matmul(
                    pW[:, q * 512:(q + 1) * 512], btokA[:, ci * 128:(ci + 1) * 128],
                    xsc[:, q * 512:(q + 1) * 512],
                    start=True, stop=True), inc=(s_t if q == 3 else None))
            t_dh = s_t.n
            kb.op("g", lambda g, tb=tb: g.tensor_tensor(
                htmp[:, :], hst[:, :], ap3(ebcA, tb, [1, 32], [0, 64]), ALU.mult), inc=s_g)
            kb.wait("v", s_g)
            kb.wait("v", s_t, t_dh)
            kb.op("v", lambda v: v.tensor_add(hst[:, :], htmp[:, :], pW[:, :]), inc=evW)

        def emit_x2T_reload(b):
            t0 = b * BLK
            kb.wait("q", d_sp0)
            kb.wait("q", d_sp1)
            kb.wait("q", s_t)
            for k in range(KD):
                kb.dma("q", lambda q, k=k, t0=t0, b=b: q.dma_start(
                    out=x2T[b % 2][:, k * BLK:(k + 1) * BLK],
                    in_=d_x2[t0:t0 + BLK, k * 128:(k + 1) * 128], transpose=True),
                    inc=d_x2t[b % 2])

        emit_x2T_reload(0)
        for b in range(NBLK):
            t0 = b * BLK
            if b + 1 < NBLK:
                emit_x2T_reload(b + 1)
            kb.wait("t", d_x2t[b % 2])
            if b == 0:
                kb.wait("t", s_v, v_halo)
                kb.wait("v", d_sp0)
                kb.wait("v", d_sp1)
                kb.wait("v", s_v, v_halo)
                for i in range(18):
                    kb.op("v", lambda v, i=i: v.tensor_copy(
                        xbcb[:, i * 515:i * 515 + 3], hal[:, i * 3:(i + 1) * 3]), inc=s_v)
            for j in range(35):
                ps, evs = (pA, evA) if j % 2 == 0 else (pB, evB)
                kb.wait("t", evs)
                for k in range(KD):
                    kb.op("t", lambda t, ps=ps, j=j, k=k, b=b, f=(k == 0), l=(k == KD - 1):
                          t.matmul(ps[:, :], W[:, k * DIP + j * 128:k * DIP + (j + 1) * 128],
                                   x2T[b % 2][:, k * BLK:(k + 1) * BLK], start=f, stop=l),
                          inc=(s_t if k == KD - 1 else None))
                if j < 16:
                    sl = j % 2
                    kb.wait("s", s_t)
                    kb.wait("s", dsp[sl])
                    kb.op("s", lambda s, ps=ps, j=j, sl=sl: s.activation(
                        zst[sl][:, :], ps[:, :], AF.Silu, bias=biasj[:, j:j + 1]), inc=evs)
                    kb.wait("g", evs)
                    kb.dma("g", lambda g, j=j, sl=sl, t0=t0: g.dma_start(
                        out=d_z[j * 128:(j + 1) * 128, t0:t0 + BLK], in_=zst[sl][:, :]),
                        inc=dsp[sl])
                elif j < 34:
                    i = j - 16
                    kb.wait("s", s_t)
                    kb.op("s", lambda s, ps=ps, j=j, i=i: s.activation(
                        xbcb[:, i * 515 + 3:i * 515 + 515], ps[:, :], AF.Identity,
                        bias=biasj[:, j:j + 1]), inc=evs)
                else:
                    kb.wait("s", s_t)
                    kb.op("s", lambda s, ps=ps: s.activation(
                        dtr[:, :], ps[0:32, :], AF.Identity, bias=b34[:, :]), inc=evs)
            kb.wait("v", evA)
            kb.wait("v", evB)
            for i in range(18):
                base = i * 515
                wof = i * 4
                sl = i % 2
                kb.wait("v", dsp[sl])
                kb.op("v", lambda v, base=base, wof=wof, sl=sl: v.tensor_scalar_mul(
                    cvt[sl][:, :], xbcb[:, base + 0:base + BLK], cwgt[:, wof:wof + 1]))
                for kk in range(1, 4):
                    src, dst = (cvt[sl], cvo[sl]) if kk % 2 == 1 else (cvo[sl], cvt[sl])
                    kb.op("v", lambda v, base=base, wof=wof, kk=kk, src=src, dst=dst:
                          v.scalar_tensor_tensor(
                              dst[:, :], xbcb[:, base + kk:base + kk + BLK],
                              cwgt[:, wof + kk:wof + kk + 1], src[:, :],
                              ALU.mult, ALU.add), inc=(s_v if kk == 3 else None))
                acc = cvo[sl] if 3 % 2 == 1 else cvt[sl]
                kb.wait("s", s_v)
                if i < 16:
                    kb.wait("s", dsp[sl])
                    kb.op("s", lambda s, acc=acc, i=i, sl=sl: s.activation(
                        cvo[sl][:, :] if acc is cvt[sl] else cvt[sl][:, :],
                        acc[:, :], AF.Silu, bias=cbia[:, i:i + 1]), inc=s_s)
                    dst = cvo[sl] if acc is cvt[sl] else cvt[sl]
                    kb.wait("g", s_s)
                    kb.dma("g", lambda g, i=i, t0=t0, dst=dst: g.dma_start(
                        out=d_xc[i * 128:(i + 1) * 128, t0:t0 + BLK], in_=dst[:, :]),
                        inc=dsp[sl])
                else:
                    tgt = BT if i == 16 else CTt
                    kb.op("s", lambda s, acc=acc, i=i, tgt=tgt, t0=t0: s.activation(
                        tgt[:, t0:t0 + BLK], acc[:, :], AF.Silu, bias=cbia[:, i:i + 1]),
                        inc=s_s)
                kb.wait("v", s_s)
                kb.op("v", lambda v, base=base: v.tensor_copy(
                    xbcb[:, base:base + 3], xbcb[:, base + BLK:base + BLK + 3]), inc=s_v)
            kb.op("s", lambda s: s.activation(tda[:, :], dtr[:, :], AF.Abs))
            kb.op("s", lambda s: s.activation(tdb[:, :], tda[:, :], AF.Exp, scale=-1.0))
            kb.op("s", lambda s: s.activation(tda[:, :], tdb[:, :], AF.Ln, bias=1.0), inc=s_s)
            kb.wait("v", s_s)
            kb.op("v", lambda v: v.tensor_scalar_max(tdb[:, :], dtr[:, :], 0.0))
            kb.op("v", lambda v: v.tensor_add(dtv[:, :], tda[:, :], tdb[:, :]))
            kb.op("v", lambda v: v.tensor_scalar_mul(avb[:, :], dtv[:, :], nac[:, :]), inc=s_v)
            kb.wait("s", s_v)
            kb.op("s", lambda s: s.activation(tda[:, :], dtv[:, :], AF.Ln), inc=s_s)
            kb.op("v", lambda v, t0=t0: v.tensor_tensor_scan(
                srel[:, t0:t0 + BLK], mres[:, :], avb[:, :], 0.0, ALU.mult, ALU.add), inc=s_v)
            kb.wait("v", s_s)
            kb.op("v", lambda v, t0=t0: v.tensor_sub(ldms[:, t0:t0 + BLK], tda[:, :],
                                                     srel[:, t0:t0 + BLK]), inc=s_v)
            sr_ap = srel.ap()
            kb.op("v", lambda v, t0=t0, b=b: v.tensor_reduce(
                dtsc[:, b:b + 1],
                bap(sr_ap.tensor, [sr_ap.ap[0], [CHK, 4], [1, 1]],
                    offset=sr_ap.offset + t0 + 127),
                AX.XY, ALU.add), inc=s_v)
            kb.wait("g", s_v)
            kb.dma("g", lambda g, t0=t0: g.dma_start(out=d_sr[:, t0:t0 + BLK],
                                                     in_=srel[:, t0:t0 + BLK]), inc=d_sp0)
            for c in range(4):
                ci = b * 4 + c
                kb.dma("g", lambda g, ci=ci: g.dma_start(
                    out=bap(d_srf.ap().tensor, [[128, 32], [1, CHK]], offset=ci * 4096),
                    in_=srel[:, ci * CHK:(ci + 1) * CHK]), inc=d_sp0)
            if b > 0:
                for c in range(4):
                    ci = (b - 1) * 4 + c
                    sl = ci % 2
                    kb.wait("q", d_sp0)
                    kb.wait("q", d_sp1)
                    kb.wait("q", s_g)
                    kb.dma("q", lambda q, ci=ci, sl=sl: q.dma_start(
                        out=xtok[sl][:, :], in_=d_xc[:, ci * CHK:(ci + 1) * CHK],
                        transpose=True), inc=d_xtk[sl])
                    kb.wait("g", d_xtk[sl])
                    emit_march(ci)
        for c in range(4):
            ci = 12 + c
            sl = ci % 2
            kb.wait("q", d_sp0)
            kb.wait("q", d_sp1)
            kb.wait("q", s_g)
            kb.dma("q", lambda q, ci=ci, sl=sl: q.dma_start(
                out=xtok[sl][:, :], in_=d_xc[:, ci * CHK:(ci + 1) * CHK], transpose=True),
                inc=d_xtk[sl])
            kb.wait("g", d_xtk[sl])
            emit_march(ci)

        kb.op("v", lambda v: v.tensor_reduce(elog[:, :], dtsc[:, :], AX.X, ALU.add),
              inc=s_v)
        kb.wait("g", evW)
        kb.dma("g", lambda g: g.dma_start(out=cD_in.ap(), in_=hst[:, :]), inc=d_cc)
        kb.wait("t", s_v)
        kb.op("t", lambda t: t.transpose(pH[0:1, 128:160], elog[:, :], idn_f[0:32, 0:32]),
              inc=s_t)
        kb.wait("v", s_t)
        kb.wait("v", d_cc)
        kb.op("v", lambda v: v.tensor_copy(payl[0:1, 0:32], pH[0:1, 128:160]), inc=s_v)
        kb.wait("g", s_v)
        kb.dma("g", lambda g: g.dma_start(out=cB_in.ap(), in_=payl[:, :]), inc=d_cc)
        kb.wait("g", d_cc)
        kb.op("g", lambda g: g.collective_compute(
            "AllGather", ALU.bypass, replica_groups=[list(range(8))],
            ins=[cB_in.ap().opt()], outs=[cB_out.ap().opt()]), inc=c_B, n=1)
        kb.op("g", lambda g: g.collective_compute(
            "AllGather", ALU.bypass, replica_groups=[[0, 1, 2, 3], [4, 5, 6, 7]],
            ins=[cD_in.ap().opt()], outs=[cD_out.ap().opt()]), inc=c_D, n=1)
        ph2.close()
        phW.close()

        kb.barrier(s_bar, dma_sems=(d_sp0, d_sp1, d_tr, d_w, d_cc, d_x))

        ph3 = ExitStack()
        xqT = ph3.enter_context(nc.sbuf_tensor("xqT", [128, KD * T], BF16))
        w1q = ph3.enter_context(nc.sbuf_tensor("w1q", [128, KD * FFN], BF16))
        w1st = [ph3.enter_context(nc.sbuf_tensor(f"w1st{i}", [128, FFN], F32)) for i in range(2)]
        qa = ph3.enter_context(nc.sbuf_tensor("qa", [128, DM], F32))
        xft = [ph3.enter_context(nc.sbuf_tensor(f"xft{i}", [128, DM], F32)) for i in range(2)]
        xqt = [ph3.enter_context(nc.sbuf_tensor(f"xqt{i}", [128, DM], BF16)) for i in range(2)]
        hso = [ph3.enter_context(nc.sbuf_tensor(f"hso{i}", [128, BLK], F32)) for i in range(2)]
        mhacc = ph3.enter_context(nc.sbuf_tensor("mhacc", [128, 128], F32))
        gB8 = ph3.enter_context(nc.sbuf_tensor("gB8", [8, 32], F32))
        gBb = ph3.enter_context(nc.sbuf_tensor("gBb", [8, 32], BF16))
        coefs = ph3.enter_context(nc.sbuf_tensor("coefs", [8, 32], F32))
        crow = ph3.enter_context(nc.sbuf_tensor("crow", [1, 32], BF16))
        cbc = ph3.enter_context(nc.sbuf_tensor("cbc", [128, 32], BF16))
        dhg = ph3.enter_context(nc.sbuf_tensor("dhg", [128, DI], BF16))

        wm8 = ph3.enter_context(nc.sbuf_tensor("wm8", [128, 8], F32))
        for k in range(KD):
            sl = k % 2
            kb.wait("g", s_v)
            kb.dma("g", lambda g, k=k, sl=sl: g.dma_start(
                out=w1st[sl][:, :], in_=w1_in[k * 128:(k + 1) * 128, :]), inc=d_w)
            kb.wait("v", d_w)
            kb.op("v", lambda v, sl=sl, k=k: v.reduce_max(wm8[:, k:k + 1], w1st[sl][:, :],
                                                          AX.X, apply_absolute_value=True),
                  inc=s_v)
        kb.wait("g", c_A, 1)
        gA8 = ph3.enter_context(nc.sbuf_tensor("gA8", [1, 8], F32))
        kb.dma("g", lambda g: g.dma_start(
            out=gA8[:, :], in_=bap(cA_out.ap().tensor, [[1, 1], [16 * 128, 8]]),
            allow_slow_non_contiguous=True), inc=d_cc)
        kb.wait("v", d_cc)
        kb.vsync(s_v, s_s)
        kb.op("v", lambda v: v.tensor_reduce(vals[0:1, 2:3], gA8[:, :], AX.X, ALU.max))
        kb.op("v", lambda v: v.tensor_reduce(stats[:, 9:10], wm8[:, :], AX.X, ALU.max),
              inc=s_v)
        kb.wait("t", s_v)
        kb.op("t", lambda t: t.transpose(pH[0:1, 0:128], stats[:, 9:10], idn_f[:, :]), inc=s_t)
        kb.vsync(s_v, s_s)
        kb.wait("v", s_t)
        kb.op("v", lambda v: v.tensor_reduce(vals[0:1, 1:2], pH[0:1, 0:128], AX.X, ALU.max))
        kb.op("v", lambda v: v.tensor_scalar_max(vals[0:1, 2:3], vals[0:1, 2:3], 1e-6),
              inc=s_v)
        kb.vsync(s_v, s_s)
        kb.op("v", lambda v: v.tensor_scalar_max(vals[0:1, 1:2], vals[0:1, 1:2], 1e-6))
        kb.op("v", lambda v: v.tensor_scalar_mul(vals[0:1, 3:4], vals[0:1, 2:3], R127),
              inc=s_v)
        kb.vsync(s_v, s_s)
        kb.op("v", lambda v: v.tensor_scalar_mul(vals[0:1, 5:6], vals[0:1, 1:2], R127))
        kb.op("v", lambda v: v.reciprocal(vals[0:1, 4:5], vals[0:1, 3:4]), inc=s_v)
        kb.vsync(s_v, s_s)
        kb.op("v", lambda v: v.reciprocal(vals[0:1, 6:7], vals[0:1, 5:6]))
        kb.op("v", lambda v: v.tensor_mul(vals[0:1, 7:8], vals[0:1, 3:4], vals[0:1, 5:6]),
              inc=s_v)
        kb.vsync(s_v, s_s)
        kb.wait("g", s_v)
        kb.dma("g", lambda g: g.dma_start(out=d_val.ap(), in_=vals[:, :]), inc=d_cc)
        kb.wait("g", d_cc)
        for slot, ci in [(4, 0), (6, 1), (7, 2)]:
            kb.dma("g", lambda g, slot=slot, ci=ci: g.dma_start(
                out=cols[:, ci:ci + 1],
                in_=bap(d_val.ap().tensor, [[0, 128], [1, 1]], offset=slot)), inc=d_cc)
        kb.wait("v", d_cc)
        kb.op("v", lambda v: v.tensor_scalar_mul(b1s[:, :], b1sb[:, :], cols[:, 2:3]), inc=s_v)
        v_sc1 = s_v.n
        for i in range(16):
            sl = i % 2
            kb.wait("q", dsp[sl])
            kb.dma("q", lambda q, i=i, sl=sl: q.dma_start(
                out=xft[sl][:, :], in_=d_xnf[i * 128:(i + 1) * 128, :]), inc=dxs[sl])
            kb.wait("s", dxs[sl])
            kb.wait("s", s_v)
            kb.op("s", lambda s, sl=sl: s.activation(
                qa[:, :], xft[sl][:, :], AF.Copy, bias=MAGIC, scale=cols[:, 0:1]), inc=s_s)
            kb.wait("v", s_s)
            kb.wait("v", dsp[sl])
            kb.op("v", lambda v, sl=sl: v.tensor_scalar(
                xqt[sl][:, :], qa[:, :], MAGIC, None, ALU.subtract), inc=s_v)
            kb.wait("g", s_v)
            kb.dma("g", lambda g, i=i, sl=sl: g.dma_start(
                out=d_xq[i * 128:(i + 1) * 128, :], in_=xqt[sl][:, :]), inc=dsp[sl])
        kb.wait("q", dsp[0])
        kb.wait("q", dsp[1])
        for k in range(KD):
            kb.dma("q", lambda q, k=k: q.dma_start(
                out=xqT[:, k * T:(k + 1) * T], in_=d_xq[:, k * 128:(k + 1) * 128],
                transpose=True), inc=d_tr)
        for k in range(KD):
            sl = k % 2
            kb.wait("g", s_v)
            kb.dma("g", lambda g, k=k, sl=sl: g.dma_start(
                out=w1st[sl][:, :], in_=w1_in[k * 128:(k + 1) * 128, :]), inc=d_w)
            kb.wait("v", d_w)
            kb.op("v", lambda v, sl=sl: v.tensor_scalar(
                w1st[1 - sl][:, :], w1st[sl][:, :], cols[:, 1:2], MAGIC, ALU.mult, ALU.add))
            kb.op("v", lambda v, k=k, sl=sl: v.tensor_scalar(
                w1q[:, k * FFN:(k + 1) * FFN], w1st[1 - sl][:, :], MAGIC, None,
                ALU.subtract), inc=s_v)
        v_w1q = s_v.n
        kb.wait("t", d_tr)
        kb.wait("s", d_tr)
        for b in range(NBLK):
            t0 = b * BLK
            for j in range(32):
                ps, evs = (pA, evA) if j % 2 == 0 else (pB, evB)
                kb.wait("t", evs)
                if b == 0 and j == 0:
                    kb.wait("t", d_tr)
                    kb.wait("t", s_v)
                for k in range(KD):
                    kb.op("t", lambda t, ps=ps, j=j, k=k, t0=t0, f=(k == 0), l=(k == KD - 1):
                          t.matmul(ps[:, :],
                                   w1q[:, k * FFN + j * 128:k * FFN + (j + 1) * 128],
                                   xqT[:, k * T + t0:k * T + t0 + BLK], start=f, stop=l),
                          inc=(s_t if k == KD - 1 else None))
                sl = j % 2
                kb.wait("s", s_t)
                kb.wait("s", dsp[sl])
                kb.wait("s", s_v)
                kb.op("s", lambda s, ps=ps, j=j, sl=sl: s.activation(
                    hso[sl][:, :], ps[:, :], AF.Gelu, bias=b1s[:, j:j + 1],
                    scale=cols[:, 2:3]), inc=evs)
                kb.wait("v", evs)
                kb.op("v", lambda e, sl=sl, j=j, b=b: e.reduce_max(
                    mhacc[:, b * 32 + j:b * 32 + j + 1], hso[sl][:, :], AX.X,
                    apply_absolute_value=True), inc=s_v)
                kb.wait("g", evs)
                kb.dma("g", lambda g, j=j, t0=t0, sl=sl: g.dma_start(
                    out=d_h[j * 128:(j + 1) * 128, t0:t0 + BLK], in_=hso[sl][:, :]),
                    inc=dsp[sl])
        kb.vsync(s_v, s_s)
        kb.op("v", lambda v: v.tensor_reduce(maxh[:, :], mhacc[:, :], AX.X, ALU.max),
              inc=s_v)
        kb.wait("t", s_v)
        kb.op("t", lambda t: t.transpose(pH[0:1, 0:128], maxh[:, :], idn_f[:, :]), inc=s_t)
        kb.wait("v", s_t)
        kb.op("v", lambda v: v.tensor_reduce(vals[0:1, 9:10], pH[0:1, 0:128], AX.X,
                                             ALU.max), inc=s_v)
        kb.vsync(s_v, s_s)
        kb.wait("s", d_cc)
        kb.wait("s", s_v)
        kb.op("s", lambda s: s.copy(payl[0:1, 0:1], vals[0:1, 9:10]))
        kb.op("s", lambda s: s.memzero(payl[0:1, 1:32]), inc=s_s)
        kb.wait("g", s_s)
        kb.dma("g", lambda g: g.dma_start(out=cC_in.ap(), in_=payl[:, :]), inc=d_cc)
        kb.wait("g", d_cc)
        kb.op("g", lambda g: g.collective_compute(
            "AllGather", ALU.bypass, replica_groups=[list(range(8))],
            ins=[cC_in.ap().opt()], outs=[cC_out.ap().opt()]), inc=c_C, n=1)

        kb.wait("g", c_B, 1)
        kb.dma("g", lambda g: g.dma_start(
            out=gB8[:, :], in_=bap(cB_out.ap().tensor, [[16 * 128, 8], [1, 32]])), inc=d_cc)
        kb.wait("v", d_cc)
        kb.op("v", lambda v: v.tensor_copy(gBb[:, :], gB8[:, :]), inc=s_v)
        kb.wait("t", s_v)
        kb.op("t", lambda t: t.matmul(pH[0:8, 0:32], mmatb[:, :], gBb[:, :],
                                      start=True, stop=True), inc=s_t)
        kb.wait("s", s_t)
        kb.op("s", lambda s: s.activation(coefs[:, :], pH[0:8, 0:32], AF.Exp), inc=s_s)
        kb.wait("v", s_s)
        kb.op("v", lambda v: v.tensor_scalar_mul(coefs[:, :], coefs[:, :], mmaskc[:, :]),
              inc=s_v)
        kb.wait("g", s_v)
        kb.dma("g", lambda g: g.dma_start(out=d_coef.ap(), in_=coefs[:, :]), inc=d_cc)
        kb.wait("g", c_D, 1)
        kb.op("g", lambda g: g.memset(sq_scr[:, 0:DI], 0.0), inc=s_g)
        kb.wait("v", s_g)
        for r in range(4):
            kb.wait("g", d_cc)
            kb.dma("g", lambda g, r=r: g.dma_start(
                out=crow[:, :], in_=d_coef[r:r + 1, :]), inc=d_cc)
            kb.wait("t", d_cc)
            kb.op("t", lambda t: t.matmul(pH[:, 192:224], onesb[:, :], crow[:, :],
                                          start=True, stop=True), inc=s_t)
            kb.wait("s", s_t)
            kb.op("s", lambda s: s.copy(cbc[:, :], pH[:, 192:224]), inc=s_s)
            kb.wait("g", s_s)
            kb.dma("g", lambda g, r=r: g.dma_start(
                out=dhg[:, :], in_=cD_out[r * 128:(r + 1) * 128, :]), inc=d_cc)
            kb.wait("v", d_cc)
            kb.wait("v", s_s)
            kb.op("v", lambda v: v.tensor_tensor(
                dhg[:, :], dhg[:, :],
                bap(cbc.ap().tensor, [cbc.ap().ap[0], [1, 32], [0, 64]],
                    offset=cbc.ap().offset), ALU.mult))
            kb.op("v", lambda v: v.tensor_add(sq_scr[:, 0:DI], sq_scr[:, 0:DI], dhg[:, :]), inc=s_v)
            kb.wait("g", s_v)
        kb.op("v", lambda v: v.tensor_copy(hst[:, :], sq_scr[:, 0:DI]), inc=s_v)

        kb.wait("g", c_C, 1)
        kb.dma("g", lambda g: g.dma_start(
            out=gC8[:, :], in_=bap(cC_out.ap().tensor, [[1, 1], [16 * 128, 8]]),
            allow_slow_non_contiguous=True), inc=d_cc)
        kb.wait("v", d_cc)
        kb.op("v", lambda v: v.tensor_reduce(vals[0:1, 11:12], gC8[:, :], AX.X, ALU.max),
              inc=s_v)
        kb.vsync(s_v, s_s)
        kb.op("v", lambda v: v.tensor_scalar_max(vals[0:1, 11:12], vals[0:1, 11:12], 1e-6),
              inc=s_v)
        kb.vsync(s_v, s_s)
        kb.op("v", lambda v: v.tensor_scalar_mul(vals[0:1, 12:13], vals[0:1, 11:12], R127),
              inc=s_v)
        kb.vsync(s_v, s_s)
        kb.op("v", lambda v: v.reciprocal(vals[0:1, 13:14], vals[0:1, 12:13]), inc=s_v)
        kb.wait("g", s_v)
        kb.dma("g", lambda g: g.dma_start(out=d_val.ap(), in_=vals[:, :]), inc=d_cc)
        kb.wait("g", d_cc)
        kb.dma("g", lambda g: g.dma_start(
            out=cols[:, 3:4], in_=bap(d_val.ap().tensor, [[0, 128], [1, 1]], offset=13)),
            inc=d_cc)
        ph3.close()

        kb.barrier(s_bar)

        ph4 = ExitStack()
        sbc = [ph4.enter_context(nc.sbuf_tensor(f"sbc{i}", [128, 4096], F32)) for i in range(2)]
        dbuf = ph4.enter_context(nc.sbuf_tensor("dbuf", [128, 4096], F32))
        gsb = ph4.enter_context(nc.sbuf_tensor("gsb", [128, 128], BF16))
        lbuf = ph4.enter_context(nc.sbuf_tensor("lbuf", [128, 4096], BF16))
        wbuf = [ph4.enter_context(nc.sbuf_tensor(f"wbuf{i}", [128, 4096], BF16))
                for i in range(2)]
        ybuf = ph4.enter_context(nc.sbuf_tensor("ybuf", [128, DI], F32))
        ygb = ph4.enter_context(nc.sbuf_tensor("ygb", [128, DI], F32))
        xtok4 = [ph4.enter_context(nc.sbuf_tensor(f"xtok4{i}", [128, DI], BF16)) for i in range(2)]
        ztok = [ph4.enter_context(nc.sbuf_tensor(f"ztok{i}", [128, DI], BF16)) for i in range(2)]
        xdb = ph4.enter_context(nc.sbuf_tensor("xdb", [128, DI], BF16))
        xsc4 = ph4.enter_context(nc.sbuf_tensor("xsc4", [128, DI], BF16))
        htmp4 = ph4.enter_context(nc.sbuf_tensor("htmp4", [128, DI], BF16))
        yrst = [ph4.enter_context(nc.sbuf_tensor(f"yrst{i}", [128, DI], BF16)) for i in range(2)]
        ssq = ph4.enter_context(nc.sbuf_tensor("ssq", [128, 4], F32))


        kb.wait("q", d_sp0)
        kb.wait("q", d_sp1)

        hpp = [ph4.enter_context(nc.sbuf_tensor(f"hpp{i}", [128, BLK], F32))
               for i in range(6)]
        qpp = [ph4.enter_context(nc.sbuf_tensor(f"qpp{i}", [128, BLK], F32))
               for i in range(2)]
        hqs = [ph4.enter_context(nc.sbuf_tensor(f"hqs{i}", [128, BLK], BF16))
               for i in range(2)]
        pp_ld, pp_st = S("pp_ld"), [S("pp_st0"), S("pp_st1")]
        pp_rn = S("pp_rn")

        def pp_load(i):
            k, tb = i // 4, (i % 4) * BLK
            kb.dma("s", lambda s, k=k, tb=tb, sl=i % 6: s.dma_start(
                out=hpp[sl][:, :], in_=d_h[k * 128:(k + 1) * 128, tb:tb + BLK]),
                inc=pp_ld)

        kb.wait("s", d_sp0)
        kb.wait("s", d_sp1)
        kb.wait("s", d_cc)
        for i in range(6):
            pp_load(i)

        def pp_tiles(ci):
            for i in range(ci * 8, ci * 8 + 8):
                k, tb, sl = i // 4, (i % 4) * BLK, i % 2
                kb.wait("s", pp_ld, 16 * (i + 1))
                kb.op("s", lambda s, i=i: s.activation(
                    qpp[i % 2][:, :], hpp[i % 6][:, :], AF.Copy, bias=MAGIC,
                    scale=cols[:, 3:4]), inc=pp_rn)
                kb.wait("s", pp_rn)
                if i + 6 < 128:
                    pp_load(i + 6)
                kb.wait("s", pp_st[sl])
                kb.op("s", lambda s, i=i: s.activation(
                    hqs[i % 2][:, :], qpp[i % 2][:, :], AF.Copy, bias=-MAGIC, scale=1.0),
                    inc=pp_rn)
                kb.wait("s", pp_rn)
                kb.dma("s", lambda s, k=k, tb=tb, sl=sl: s.dma_start(
                    out=d_hq[k * 128:(k + 1) * 128, tb:tb + BLK], in_=hqs[sl][:, :]),
                    inc=pp_st[sl])

        def p4_prefetch(ci):
            sl = ci % 2
            c0 = ci * CHK
            kb.wait("q", s_g)
            kb.wait("q", s_v)
            kb.dma("q", lambda q, ci=ci, sl=sl: q.dma_start(
                out=sbc[sl][:, :],
                in_=bap(d_srf.ap().tensor, [[0, 128], [1, 4096]], offset=ci * 4096)),
                inc=d_xtk[sl])
            kb.dma("q", lambda q, c0=c0, sl=sl: q.dma_start(
                out=xtok4[sl][:, :], in_=d_xc[:, c0:c0 + CHK], transpose=True),
                inc=d_xtk[sl])
            kb.dma("q", lambda q, c0=c0, sl=sl: q.dma_start(
                out=ztok[sl][:, :], in_=d_z[:, c0:c0 + CHK], transpose=True),
                inc=d_xtk[sl])

        def p4_head(ci):
            sl = ci % 2
            tb = ci * 32
            c0 = ci * CHK
            kb.wait("t", evB)
            kb.op("t", lambda t, c0=c0: t.matmul(pB[:, 0:128], BT[:, c0:c0 + CHK],
                                                 CTt[:, c0:c0 + CHK], start=True, stop=True),
                  inc=s_t)
            kb.wait("s", s_t)
            kb.wait("s", s_v)
            kb.op("s", lambda s: s.copy(gsb[:, :], pB[:, 0:128]), inc=evB)
            kb.wait("v", d_xtk[sl])
            HB = 4096
            NHH = 1
            for hh in range(NHH):
                o = hh * HB
                NHD = 32 // NHH
                kb.op("v", lambda v, sl=sl, tb=tb, o=o, NHD=NHD: v.tensor_tensor(
                    bap(dbuf.ap().tensor, [dbuf.ap().ap[0], [CHK, NHD], [1, CHK]],
                        offset=dbuf.ap().offset + o),
                    bap(sbc[sl].ap().tensor, [sbc[sl].ap().ap[0], [CHK, NHD], [1, CHK]],
                        offset=sbc[sl].ap().offset + o),
                    ap3(tcolA, tb + hh * NHD, [1, NHD], [0, CHK]), ALU.add), inc=s_v)
                kb.wait("g", s_v)
                kb.op("g", lambda g, o=o, NHD=NHD: g.affine_select(
                    lbuf[:, o:o + HB], dbuf[:, o:o + HB], [[0, NHD], [1, CHK]],
                    ALU.is_ge, NEG, base=0, channel_multiplier=-1), inc=s_g)
                kb.wait("s", s_g)
                kb.op("s", lambda s, o=o: s.activation(lbuf[:, o:o + HB], lbuf[:, o:o + HB],
                                                       AF.Exp), inc=s_s)
                kb.wait("v", evB)
                kb.wait("v", s_s)
                kb.op("v", lambda v, sl=sl, o=o, NHD=NHD: v.tensor_tensor(
                    wbuf[sl][:, o:o + HB], lbuf[:, o:o + HB],
                    bap(gsb.ap().tensor, [gsb.ap().ap[0], [0, NHD], [1, CHK]],
                        offset=gsb.ap().offset), ALU.mult), inc=s_v)
            wmark[ci % 2] = s_v.n

        wmark = {0: 0, 1: 0}
        p4_prefetch(0)
        p4_head(0)
        for ci in range(NCHK):
            sl = ci % 2
            c0 = ci * CHK
            tb = ci * 32
            if ci + 1 < NCHK:
                p4_prefetch(ci + 1)
            kb.wait("t", evW)
            kb.wait("t", evA)
            kb.wait("t", evS)
            for q in range(4):
                kb.op("t", lambda t, c0=c0, q=q: t.matmul(
                    pW[:, q * 512:(q + 1) * 512], CTt[:, c0:c0 + CHK],
                    hst[:, q * 512:(q + 1) * 512], start=True, stop=True),
                    inc=(s_t if q == 3 else None))
            kb.wait("g", d_xtk[sl])
            kb.op("g", lambda g, sl=sl: g.tensor_tensor(
                xdb[:, :], xtok4[sl][:, :],
                bap(DT32.ap().tensor, [DT32.ap().ap[0], [1, 32], [0, 64]],
                    offset=DT32.ap().offset), ALU.mult), inc=s_g)
            g_xd = s_g.n
            kb.op("g", lambda g, sl=sl, tb=tb: g.tensor_tensor(
                xsc4[:, :], xtok4[sl][:, :], ap3(e2tA, tb, [1, 32], [0, 64]), ALU.mult),
                inc=s_g)
            kb.wait("g", s_t)
            kb.op("g", lambda g, tb=tb: g.tensor_tensor(
                htmp4[:, :], hst[:, :], ap3(ebcA, tb, [1, 32], [0, 64]), ALU.mult), inc=s_g)
            kb.wait("v", s_t)
            kb.op("v", lambda v, tb=tb: v.tensor_tensor(
                ybuf[:, :], pW[:, :], ap3(estA, tb, [1, 32], [0, 64]), ALU.mult), inc=evW)
            kb.wait("t", evW)
            kb.wait("t", s_v, wmark[sl])
            kb.wait("t", s_g, g_xd)
            for h in range(NH):
                kb.op("t", lambda t, h=h, sl=sl: t.matmul(
                    pW[:, h * 64:(h + 1) * 64], wbuf[sl][:, h * 128:(h + 1) * 128],
                    xtok4[sl][:, h * 64:(h + 1) * 64], start=(h % 8 == 0), stop=False))
            for q in range(4):
                kb.op("t", lambda t, q=q: t.matmul(
                    pW[:, q * 512:(q + 1) * 512], idn_b[:, :], xdb[:, q * 512:(q + 1) * 512],
                    start=False, stop=True), inc=(s_t if q == 3 else None))
            kb.wait("t", evA)
            kb.wait("t", evS)
            kb.wait("t", s_g)
            for q in range(2):
                ps_q = pA if q == 0 else pS
                kb.op("t", lambda t, ci=ci, q=q, ps_q=ps_q: t.matmul(
                    ps_q[:, :], btokA[:, ci * 128:(ci + 1) * 128],
                    xsc4[:, q * 512:(q + 1) * 512], start=True, stop=True), inc=s_t)
            t_dh01 = s_t.n
            kb.wait("v", s_t)
            kb.op("v", lambda v: v.tensor_add(ybuf[:, :], ybuf[:, :], pW[:, :]), inc=evW)
            kb.wait("v", s_g)
            kb.op("v", lambda v: v.tensor_add(hst[:, 0:512], htmp4[:, 0:512], pA[:, :]),
                  inc=evA)
            kb.op("v", lambda v: v.tensor_add(hst[:, 512:1024], htmp4[:, 512:1024],
                                              pS[:, :]), inc=evS)
            for q in range(2, 4):
                ps_q, ev_q = (pA, evA) if q % 2 == 0 else (pS, evS)
                kb.wait("t", ev_q)
                kb.op("t", lambda t, ci=ci, q=q, ps_q=ps_q: t.matmul(
                    ps_q[:, :], btokA[:, ci * 128:(ci + 1) * 128],
                    xsc4[:, q * 512:(q + 1) * 512], start=True, stop=True), inc=s_t)
                t_dhq = s_t.n
                kb.wait("v", s_t, t_dhq)
                kb.op("v", lambda v, q=q, ps_q=ps_q: v.tensor_add(
                    hst[:, q * 512:(q + 1) * 512], htmp4[:, q * 512:(q + 1) * 512],
                    ps_q[:, :]), inc=ev_q)
            if ci + 1 < NCHK:
                p4_head(ci + 1)
            kb.wait("g", evW)
            kb.op("g", lambda g, sl=sl: g.tensor_mul(ygb[:, :], ybuf[:, :], ztok[sl][:, :]),
                  inc=s_g)
            kb.wait("s", s_g)
            kb.op("s", lambda s: s.activation(sq_scr[:, :], ygb[:, :], AF.Square,
                                              accum_out=ssq[:, 0:1]))
            kb.op("s", lambda s: s.activation(ssq[:, 1:2], ssq[:, 0:1], AF.Ln,
                                              bias=EPS, scale=1.0 / DI))
            kb.op("s", lambda s: s.activation(ssq[:, 2:3], ssq[:, 1:2], AF.Exp,
                                              scale=-0.5), inc=s_s)
            kb.wait("s", dsp[sl])
            kb.op("s", lambda s, sl=sl: s.activation(yrst[sl][:, :], ygb[:, :], AF.Identity,
                                                     scale=ssq[:, 2:3]), inc=s_s)
            kb.wait("g", s_s)
            kb.dma("g", lambda g, c0=c0, sl=sl: g.dma_start(
                out=d_yr[c0:c0 + CHK, :], in_=yrst[sl][:, :]), inc=dsp[sl])
            pp_tiles(ci)
        ph4.close()
        phT.close()
        kb.barrier(s_bar, dma_sems=(d_sp0, d_sp1, d_tr, d_w, d_cc, pp_ld,
                                    pp_st[0], pp_st[1]))

        sT = sb("sT", [128, 8 * T], BF16)
        ph5 = ExitStack()
        xnT2 = ph5.enter_context(nc.sbuf_tensor("xnT2", [128, KD * T], BF16))
        woq = ph5.enter_context(nc.sbuf_tensor("woq", [128, 16 * DM], BF16))
        yrT = [ph5.enter_context(nc.sbuf_tensor(f"yrT{i}", [128, 16 * BLK], BF16))
               for i in range(2)]
        w2sc = [ph5.enter_context(nc.sbuf_tensor(f"w2sc{i}", [128, BLK], F32))
                for i in range(8)]
        d_w2s, sm_w2 = S("d_w2s"), S("sm_w2")
        SC = 64

        def w2load(j):
            k, hf = j // 2, j % 2
            kb.dma("s", lambda s, k=k, hf=hf, sl=j % 8: s.dma_start(
                out=w2sc[sl][:, :],
                in_=w2_in[k * 128:(k + 1) * 128, hf * 512:(hf + 1) * 512]), inc=d_w2s)

        for j in range(8):
            w2load(j)

        def w2scan_pairs(b):
            for j in range(b * 16, b * 16 + 16):
                k, hf = j // 2, j % 2
                wmT = wm32 if hf == 0 else wmB
                kb.wait("v", d_w2s, 16 * (j + 1))
                kb.op("v", lambda v, j=j, k=k, wmT=wmT: v.reduce_max(
                    wmT[:, k:k + 1], w2sc[j % 8][:, :], AX.X,
                    apply_absolute_value=True), inc=sm_w2)
                if j + 8 < SC:
                    kb.wait("s", sm_w2, j + 1)
                    w2load(j + 8)
        kb.wait("q", d_sp0)
        kb.wait("q", d_sp1)
        for k in range(KD):
            kb.dma("q", lambda q, k=k: q.dma_start(
                out=xnT2[:, k * T:(k + 1) * T], in_=d_xn[:, k * 128:(k + 1) * 128],
                transpose=True), inc=d_tr)
        for k in range(16):
            kb.dma("g", lambda g, k=k: g.dma_start(
                out=woq[:, k * DM:(k + 1) * DM], in_=wo_in[k * 128:(k + 1) * 128, :]),
                inc=d_w)
        kb.wait("v", d_w)
        for k in range(16):
            kb.op("v", lambda v, k=k: v.tensor_scalar_mul(
                woq[:, k * DM:(k + 1) * DM], woq[:, k * DM:(k + 1) * DM], nwc[:, k:k + 1]),
                inc=s_v)
        for b in range(NBLK):
            t0 = b * BLK
            sl = b % 2
            kb.wait("q", evA)
            kb.wait("q", evB)
            kb.wait("q", dsp[0])
            kb.wait("q", dsp[1])
            for k in range(16):
                kb.dma("q", lambda q, k=k, t0=t0, sl=sl: q.dma_start(
                    out=yrT[sl][:, k * BLK:(k + 1) * BLK],
                    in_=d_yr[t0:t0 + BLK, k * 128:(k + 1) * 128], transpose=True), inc=d_tr)
            kb.wait("t", d_tr)
            for d in range(8):
                ps, evs = (pA, evA) if d % 2 == 0 else (pB, evB)
                kb.wait("t", evs)
                for k in range(16):
                    kb.op("t", lambda t, ps=ps, d=d, k=k, sl=sl, f=(k == 0), l=(k == 15):
                          t.matmul(ps[:, :],
                                   woq[:, k * DM + d * 128:k * DM + (d + 1) * 128],
                                   yrT[sl][:, k * BLK:(k + 1) * BLK], start=f, stop=l),
                          inc=(s_t if k == 15 else None))
                kb.wait("v", s_t)
                kb.op("v", lambda v, ps=ps, d=d, t0=t0: v.tensor_add(
                    sT[:, d * T + t0:d * T + t0 + BLK], ps[:, :],
                    xnT2[:, d * T + t0:d * T + t0 + BLK]), inc=evs)
            w2scan_pairs(b)
        if "sres" in TAPS:
            kb.wait("g", evA)
            kb.wait("g", evB)
            for d in range(8):
                kb.dma("g", lambda g, d=d: g.dma_start(
                    out=d_sdbg[d * 128:(d + 1) * 128, :], in_=sT[:, d * T:(d + 1) * T]),
                    inc=d_sp0)
        ph5.close()

        kb.barrier(s_bar, dma_sems=(d_sp0, d_sp1, d_tr, d_w, d_cc, d_w2s))

        ph6 = ExitStack()
        w2q = ph6.enter_context(nc.sbuf_tensor("w2q", [128, 32 * DM], BF16))
        w2st = [ph6.enter_context(nc.sbuf_tensor(f"w2st{i}", [128, DM], F32)) for i in range(2)]
        hqb = ph6.enter_context(nc.sbuf_tensor("hqb", [128, 32 * BLK], BF16))
        gstg = ph6.enter_context(nc.sbuf_tensor("gstg", [128, BLK], F32))
        fstg = [ph6.enter_context(nc.sbuf_tensor(f"fstg{i}", [128, BLK], BF16)) for i in range(2)]
        fT = [ph6.enter_context(nc.sbuf_tensor(f"fT{i}", [128, DM], BF16, is_transpose_dest=True)
                                ) for i in range(2)]
        xin6 = [ph6.enter_context(nc.sbuf_tensor(f"xin6{i}", [128, DM], F32)) for i in range(2)]


        kb.wait("v", sm_w2)
        kb.op("v", lambda v: v.tensor_reduce(stats[:, 9:10], wm32[:, :], AX.X, ALU.max),
              inc=s_v)
        kb.op("v", lambda v: v.tensor_reduce(stats[:, 10:11], wmB[:, :], AX.X, ALU.max),
              inc=s_v)
        kb.vsync(s_v, s_s)
        kb.op("v", lambda v: v.scalar_tensor_tensor(
            stats[:, 9:10], stats[:, 10:11], 1.0, stats[:, 9:10],
            ALU.mult, ALU.max), inc=s_v)
        kb.wait("t", s_v)
        kb.op("t", lambda t: t.transpose(pH[0:1, 0:128], stats[:, 9:10], idn_f[:, :]), inc=s_t)
        kb.wait("v", s_t)
        kb.op("v", lambda v: v.tensor_reduce(vals[0:1, 10:11], pH[0:1, 0:128], AX.X, ALU.max),
              inc=s_v)
        kb.vsync(s_v, s_s)
        kb.op("v", lambda v: v.tensor_scalar_max(vals[0:1, 10:11], vals[0:1, 10:11], 1e-6),
              inc=s_v)
        kb.vsync(s_v, s_s)
        kb.op("v", lambda v: v.tensor_scalar_mul(vals[0:1, 14:15], vals[0:1, 10:11], R127),
              inc=s_v)
        kb.vsync(s_v, s_s)
        kb.op("v", lambda v: v.reciprocal(vals[0:1, 15:16], vals[0:1, 14:15]))
        kb.op("v", lambda v: v.tensor_mul(vals[0:1, 9:10], vals[0:1, 12:13],
                                          vals[0:1, 14:15]), inc=s_v)
        kb.vsync(s_v, s_s)
        kb.wait("g", s_v)
        kb.dma("g", lambda g: g.dma_start(out=d_val.ap(), in_=vals[:, :]), inc=d_cc)
        kb.wait("g", d_cc)
        for slot, ci in [(15, 4), (9, 5)]:
            kb.dma("g", lambda g, slot=slot, ci=ci: g.dma_start(
                out=cols[:, ci:ci + 1],
                in_=bap(d_val.ap().tensor, [[0, 128], [1, 1]], offset=slot)), inc=d_cc)
        kb.wait("v", d_cc)
        kb.op("v", lambda v: v.tensor_scalar_mul(b2s[:, :], b2sb[:, :], cols[:, 5:6]), inc=s_v)
        for k in range(32):
            sl = k % 2
            kb.wait("g", s_v)
            kb.dma("g", lambda g, k=k, sl=sl: g.dma_start(
                out=w2st[sl][:, :], in_=w2_in[k * 128:(k + 1) * 128, :]), inc=d_w)
            kb.wait("v", d_w)
            kb.op("v", lambda v, sl=sl: v.tensor_scalar(
                w2st[1 - sl][:, :], w2st[sl][:, :], cols[:, 4:5], MAGIC, ALU.mult, ALU.add))
            kb.op("v", lambda v, k=k, sl=sl: v.tensor_scalar(
                w2q[:, k * DM:(k + 1) * DM], w2st[1 - sl][:, :], MAGIC, None, ALU.subtract),
                inc=s_v)
        pp_q = [S("pp_q0"), S("pp_q1")]
        kb.wait("t", s_v)
        for b in range(NBLK):
            t0 = b * BLK
            if b > 0:
                kb.wait("q", s_t)
                kb.wait("s", s_t)
            for k in range(32):
                eng, qs = ("q", pp_q[0]) if k % 2 == 0 else ("s", pp_q[1])
                kb.dma(eng, lambda e, k=k, t0=t0: e.dma_start(
                    out=hqb[:, k * BLK:(k + 1) * BLK],
                    in_=d_hq[k * 128:(k + 1) * 128, t0:t0 + BLK]), inc=qs)
            kb.wait("t", pp_q[0])
            kb.wait("t", pp_q[1])
            for d in range(8):
                ps, evs = (pA, evA) if d % 2 == 0 else (pB, evB)
                kb.wait("t", evs)
                for k in range(32):
                    kb.op("t", lambda t, ps=ps, d=d, k=k, f=(k == 0), l=(k == 31):
                          t.matmul(ps[:, :],
                                   w2q[:, k * DM + d * 128:k * DM + (d + 1) * 128],
                                   hqb[:, k * BLK:(k + 1) * BLK], start=f, stop=l),
                          inc=(s_t if k == 31 else None))
                sl = d % 2
                kb.wait("s", s_t)
                kb.op("s", lambda s, ps=ps, d=d: s.activation(
                    gstg[:, :], ps[:, :], AF.Silu, bias=b2s[:, d:d + 1], scale=cols[:, 5:6]),
                    inc=s_s)
                kb.wait("v", s_s)
                kb.wait("v", dsp[sl])
                kb.op("v", lambda v, d=d, t0=t0, sl=sl: v.tensor_mul(
                    fstg[sl][:, :], gstg[:, :], sT[:, d * T + t0:d * T + t0 + BLK]), inc=evs)
                kb.wait("g", evs)
                kb.dma("g", lambda g, d=d, t0=t0, sl=sl: g.dma_start(
                    out=d_F[d * 128:(d + 1) * 128, t0:t0 + BLK], in_=fstg[sl][:, :]),
                    inc=dsp[sl])
        fin_ld = [d_ld0, d_ld1]
        fin_st = [S("fin_st0"), S("fin_st1")]
        kb.wait("q", dsp[0])
        kb.wait("q", dsp[1])
        for ci in range(NCHK):
            c0 = ci * CHK
            sl = ci % 2
            kb.wait("q", fin_st[sl])
            kb.dma("q", lambda q, c0=c0, sl=sl: q.dma_start(
                out=fT[sl][:, :], in_=d_F[:, c0:c0 + CHK], transpose=True), inc=fin_ld[sl])
            kb.dma("q", lambda q, c0=c0, sl=sl: q.dma_start(
                out=xin6[sl][:, :], in_=x_in[c0:c0 + CHK, :]), inc=fin_ld[sl])
            kb.wait("v", fin_ld[sl])
            kb.wait("v", fin_st[sl])
            kb.op("v", lambda v, sl=sl: v.tensor_add(xin6[sl][:, :], xin6[sl][:, :],
                                                     fT[sl][:, :]), inc=s_v)
            kb.wait("g", s_v)
            kb.dma("g", lambda g, c0=c0, sl=sl: g.dma_start(
                out=out_t[c0:c0 + CHK, :], in_=xin6[sl][:, :]), inc=fin_st[sl])
            fin_st[sl].n
            s_out.n
        kb.wait("q", fin_st[0])
        kb.wait("q", fin_st[1])
        for name, tup in taps.items():
            kb.wait("g", fin_st[0])
            kb.wait("g", fin_st[1])
            if len(tup) == 3:
                tout, src, row = tup
                kb.dma("g", lambda g, tout=tout, src=src, row=row: g.dma_start(
                    out=tout.ap(), in_=src[row:row + 128, :]), inc=s_tap)
            else:
                tout, src = tup
                kb.dma("g", lambda g, tout=tout, src=src: g.dma_start(
                    out=tout.ap(), in_=src.ap()), inc=s_tap)
        kb.wait("g", s_tap)
        ph6.close()

        with nc.Block() as block:
            block.tensor(lambda e: [f(e) for f in kb.streams["t"]])
            block.vector(lambda e: [f(e) for f in kb.streams["v"]])
            block.scalar(lambda e: [f(e) for f in kb.streams["s"]])
            block.gpsimd(lambda e: [f(e) for f in kb.streams["g"]])
            block.sync(lambda e: [f(e) for f in kb.streams["q"]])

    return nc


_NC_CACHE = {}


def _get_nc():
    key = tuple(TAPS)
    if key not in _NC_CACHE:
        _NC_CACHE[key] = build_nc()
    return _NC_CACHE[key]


def make_inmaps(inputs):
    x = np.asarray(inputs["L0_attn_x"], np.float32)
    f32 = lambda a: np.ascontiguousarray(np.asarray(a, np.float32))
    w_inT = f32(np.pad(np.asarray(inputs["L0_attn_in_proj_w"], np.float32).T,
                       ((0, 0), (0, DIP - 4384))))
    woT = f32(np.asarray(inputs["L0_attn_out_proj_w"], np.float32).T)
    w1T = f32(np.asarray(inputs["L0_ffn_w1"], np.float32).T)
    w2T = f32(np.asarray(inputs["L0_ffn_w2"], np.float32).T)
    convw = f32(np.asarray(inputs["L0_attn_conv_w"], np.float32)
                .reshape(18, 128, 4).transpose(1, 0, 2).reshape(128, 72))
    convb = f32(np.asarray(inputs["L0_attn_conv_b"], np.float32).reshape(18, 128).T)
    lnw = f32(np.asarray(inputs["L0_attn_ln_w"], np.float32).reshape(KD, 128).T)
    lnb = f32(np.asarray(inputs["L0_attn_ln_b"], np.float32).reshape(KD, 128).T)
    dtb = f32(np.asarray(inputs["L0_attn_dt_bias"], np.float32).reshape(32, 1))
    alog = f32(np.asarray(inputs["L0_attn_A_log"], np.float32).reshape(32, 1))
    dvec = f32(np.asarray(inputs["L0_attn_D"], np.float32).reshape(1, 32))
    nw = f32(np.asarray(inputs["L0_attn_norm_w"], np.float32).reshape(16, 128).T)
    b1c = f32(np.asarray(inputs["L0_ffn_b1"], np.float32).reshape(32, 128).T)
    b2c = f32(np.asarray(inputs["L0_ffn_b2"], np.float32).reshape(8, 128).T)
    in_maps = []
    for k in range(8):
        b, s = k // 4, k % 4
        xs = f32(x[b, s * T:(s + 1) * T])
        if s == 0:
            xh = np.zeros((3, DM), np.float32)
            hm = np.zeros((1, 1), np.float32)
        else:
            xh = f32(x[b, s * T - 3:s * T])
            hm = np.ones((1, 1), np.float32)
        mm = np.zeros((8, 8), np.float32)
        mk = np.zeros((8, 1), np.float32)
        for r in range(s):
            mk[r, 0] = 1.0
            for i in range(r + 1, s):
                mm[4 * b + i, r] = 1.0
        in_maps.append({
            "x": xs, "xh": xh, "hm": hm, "mmat": mm, "mmask": mk,
            "w_in": w_inT, "wo": woT, "w1": w1T, "w2": w2T,
            "convw": convw, "convb": convb, "lnw": lnw, "lnb": lnb,
            "dtb": dtb, "alog": alog, "dvec": dvec, "nw": nw,
            "b1c": b1c, "b2c": b2c,
        })
    return in_maps


def kernel(**inputs):
    nc = _get_nc()
    in_maps = make_inmaps(inputs)
    res = run_bass_kernel_spmd(nc, in_maps, core_ids=list(range(8)))
    x = np.asarray(inputs["L0_attn_x"], np.float32)
    out = np.empty_like(x)
    for k in range(8):
        b, s = k // 4, k % 4
        out[b, s * T:(s + 1) * T] = np.asarray(res.results[k]["out"], np.float32)
    return out
